# version 04
# baseline (speedup 1.0000x reference)
import numpy as np
import ml_dtypes
import concourse.bass as bass
import concourse.mybir as mybir
from concourse.bass_utils import run_bass_kernel_spmd

F32 = mybir.dt.float32
BF16 = mybir.dt.bfloat16
AF = mybir.ActivationFunctionType
ALU = mybir.AluOpType

B, S, D, HF = 8, 2048, 2048, 8192
NDT = D // 128
NHT = HF // 128
SH = S // 2
HD = 64
THETA = 10000.0
EPS = 1e-5
C_FWHT = 0.5 / float(np.sqrt(HF))
BF = ml_dtypes.bfloat16
KB = 1024

OFF_HID = 0
OFF_XTF = 0
OFF_XB16 = 64 * KB
OFF_XSQ = 96 * KB
OFF_MEANB = 128 * KB
OFF_RSTDB = 132 * KB
OFF_S1B = 0
OFF_USB = 8 * KB
OFF_V2B = 16 * KB
OFF_P2SB = 24 * KB
OFF_COSS = 0
OFF_SINS = 8 * KB
OFF_KT = 16 * KB
OFF_VAUG = 32 * KB
OFF_WVB = 49 * KB
OFF_QBUF = 49 * KB
OFF_PRB = 51 * KB
OFF_AVS = 55 * KB
OFF_RC = 59 * KB
OFF_RC16 = 63 * KB
OFF_OUTT = 65 * KB
OFF_XNT = 137 * KB
OFF_XRES = 137 * KB
OFF_XNEW = 153 * KB
OFF_W1B = 169 * KB
OFF_W3B = 177 * KB
OFF_TMPS = 185 * KB
OFF_PBUF = 189 * KB
OFF_STA = 169 * KB
OFF_STB = 173 * KB
OFF_STC = 177 * KB
OFF_ST16 = 181 * KB
OFF_QRAW = 177 * KB
OFF_QP = 179 * KB
OFF_T1 = 181 * KB
OFF_T2 = 185 * KB
OFF_CONST = 198 * KB


def _hadamard(n):
    i = np.arange(n)
    m = i[:, None] & i[None, :]
    bits = np.zeros_like(m)
    mm = m.copy()
    while mm.any():
        bits += mm & 1
        mm >>= 1
    return np.where(bits % 2 == 0, 1.0, -1.0).astype(np.float32)


def _rope_tables():
    inv_freq = 1.0 / THETA ** (np.arange(0, HD, 2, dtype=np.float32) / HD)
    pos = np.arange(S, dtype=np.float32)
    i = np.arange(128)
    f = inv_freq[(i % 64) % 32]
    ang = pos[None, :] * f[:, None]
    cosT = np.cos(ang).astype(np.float32)
    sgn = np.where((i % 64) < 32, -1.0, 1.0).astype(np.float32)
    sinS = (np.sin(ang) * sgn[:, None]).astype(np.float32)
    return cosT, sinS


def _rot_matrix():
    R = np.zeros((128, 128), np.float32)
    for m in range(128):
        R[(m // 64) * 64 + ((m % 64) ^ 32), m] = 1.0
    return R


def _wt_chunks(w, nct):
    a = w.reshape(nct, 128, NDT, 128)
    return np.ascontiguousarray(a.transpose(0, 3, 2, 1)).reshape(nct, 128, NDT * 128)


def host_prepare(inp):
    out = {}
    jp = np.arange(HF)
    hperm = (jp % 64) * 128 + (jp // 64)
    out["a1"] = _wt_chunks(np.asarray(inp["L0_ffn_W1"])[hperm, :], NHT)
    out["a3"] = _wt_chunks(np.asarray(inp["L0_ffn_W3"])[hperm, :], NHT)
    out["w2t"] = np.ascontiguousarray(np.asarray(inp["L0_ffn_W2"]).T)
    out["aq"] = _wt_chunks(np.asarray(inp["L0_attn_Wq"]), 16)
    out["ak"] = _wt_chunks(np.asarray(inp["L0_attn_Wk"]), 4)
    out["av"] = np.ascontiguousarray(np.asarray(inp["L0_attn_Wv"]).T.reshape(NDT, 128, 512))
    wo = np.asarray(inp["L0_attn_Wo"]).reshape(NDT, 128, NDT, 128)
    out["ao"] = np.ascontiguousarray(wo.transpose(0, 3, 2, 1)).reshape(NDT, 128, NDT * 128)
    out["h128"] = _hadamard(128).astype(BF)
    out["h64"] = _hadamard(64).astype(BF)
    cosT, sinS = _rope_tables()
    out["cosT"] = cosT
    out["sinS"] = sinS
    out["rotm"] = _rot_matrix().astype(BF)
    out["tril"] = np.triu(np.ones((128, 128), np.float32)).astype(BF)
    return out


def build_nc(debug_taps=(), stages="full"):
    nc = bass.Bass("TRN2", target_bir_lowering=False, debug=False)
    uid = [0]

    rbase = nc.bump_sbuf(201 * KB)[0]
    rbase = (rbase + 127) // 128 * 128

    def sb(name, shape, dtype, off):
        uid[0] += 1
        return nc.alloc_sbuf_tensor_at(f"{name}{uid[0]}", list(shape), dtype, offset=rbase + off)

    class SemH:

        def __init__(self, handle):
            self.h = handle
            self.base = 0
            self.pending = 0

        def add(self, n):
            self.pending += n
            return n

    sem_pool = {}

    def sems(*names):
        out = []
        for n in names:
            if n not in sem_pool:
                sem_pool[n] = SemH(nc.alloc_semaphore(f"sem_{n}"))
            out.append(sem_pool[n])
        return out

    def end_phase():
        nc.all_engine_barrier()
        for s in sem_pool.values():
            s.base += s.pending
            s.pending = 0

    class Chain:

        def __init__(self, sem):
            self.sem = sem

        def seal(self, ins):
            ins.then_inc(self.sem.h, self.sem.add(1))
            return ins

        def wait(self, eng):
            eng.wait_ge(self.sem.h, self.sem.base + self.sem.pending)

    xin = nc.dram_tensor("xin", [D, S], F32, kind="ExternalInput").ap()
    a1 = nc.dram_tensor("a1", [NHT, 128, NDT * 128], F32, kind="ExternalInput").ap()
    a3 = nc.dram_tensor("a3", [NHT, 128, NDT * 128], F32, kind="ExternalInput").ap()
    w2t = nc.dram_tensor("w2t", [HF, D], F32, kind="ExternalInput").ap()
    aq = nc.dram_tensor("aq", [16, 128, NDT * 128], F32, kind="ExternalInput").ap()
    ak = nc.dram_tensor("ak", [4, 128, NDT * 128], F32, kind="ExternalInput").ap()
    av = nc.dram_tensor("av", [NDT, 128, 512], F32, kind="ExternalInput").ap()
    ao = nc.dram_tensor("ao", [NDT, 128, NDT * 128], F32, kind="ExternalInput").ap()
    h128_in = nc.dram_tensor("h128", [128, 128], BF16, kind="ExternalInput").ap()
    h64_in = nc.dram_tensor("h64", [64, 64], BF16, kind="ExternalInput").ap()
    cosT_in = nc.dram_tensor("cosT", [128, S], F32, kind="ExternalInput").ap()
    sinS_in = nc.dram_tensor("sinS", [128, S], F32, kind="ExternalInput").ap()
    rotm_in = nc.dram_tensor("rotm", [128, 128], BF16, kind="ExternalInput").ap()
    tril_in = nc.dram_tensor("tril", [128, 128], BF16, kind="ExternalInput").ap()

    u_s = nc.dram_tensor("u_s", [HF, D], BF16, kind="Internal").ap()
    p4 = nc.dram_tensor("p4", [HF, D], BF16, kind="Internal").ap()
    xa = nc.dram_tensor("xa", [D, S], F32, kind="Internal").ap()
    xb = nc.dram_tensor("xb", [D, S], F32, kind="Internal").ap()
    qs = nc.dram_tensor("qs", [16, 128, S], BF16, kind="Internal").ap()
    den_s = nc.dram_tensor("den_s", [128, 512], F32, kind="Internal").ap()
    rc_s = nc.dram_tensor("rc_s", [128, 512], F32, kind="Internal").ap()
    yT = nc.dram_tensor("yT", [D, S], F32, kind="ExternalOutput").ap()

    pall = nc.alloc_psum_tensor("pall", [128, 4096], F32)

    def bank(b):
        return pall[:, 512 * b:512 * (b + 1)]

    ones_col = sb("ones_col", [128, 1], BF16, OFF_CONST)
    ones_row = sb("ones_row", [1, 128], BF16, OFF_CONST + 64)
    h128 = sb("h128c", [128, 128], BF16, OFF_CONST + 384)
    h64 = sb("h64c", [64, 64], BF16, OFF_CONST + 640)
    rotm = sb("rotmc", [128, 128], BF16, OFF_CONST + 768)
    tril = sb("trilc", [128, 128], BF16, OFF_CONST + 1024)

    XNT = sb("xnT", [128, NDT * SH], BF16, OFF_XNT)
    HID = sb("hid", [128, NHT * SH], BF16, OFF_HID)
    W1B = sb("w1b", [128, 2 * 2048], BF16, OFF_W1B)
    W3B = sb("w3b", [128, 2 * 2048], BF16, OFF_W3B)
    TMPS = sb("tmps", [128, 4 * 512], BF16, OFF_TMPS)
    PBUF = sb("pbuf", [128, 2 * 2048], BF16, OFF_PBUF)
    XRES = sb("xres", [128, 8 * 512], F32, OFF_XRES)
    XNEW = sb("xnew", [128, 8 * 512], F32, OFF_XNEW)
    COSS = sb("cosS", [128, S], F32, OFF_COSS)
    SINSS = sb("sinS", [128, S], F32, OFF_SINS)
    KTT = sb("kT", [128, 4 * S], BF16, OFF_KT)
    VAUG = sb("vaug", [128, 16 * 520], BF16, OFF_VAUG)
    WVB = sb("wvb", [128, HF], BF16, OFF_WVB)
    OUTT = sb("outT", [128, NDT * S], BF16, OFF_OUTT)

    (s_c,) = sems("c")
    with nc.Block() as blk:
        @blk.gpsimd
        def _(g):
            g.dma_start(h128[:], h128_in).then_inc(s_c.h, s_c.add(16))
            g.dma_start(h64[:], h64_in).then_inc(s_c.h, s_c.add(16))
            g.dma_start(rotm[:], rotm_in).then_inc(s_c.h, s_c.add(16))
            g.dma_start(tril[:], tril_in).then_inc(s_c.h, s_c.add(16))
            g.memset(ones_col[:], 1.0)
            g.memset(ones_row[:], 1.0)
            g.wait_ge(s_c.h, s_c.base + (64))
    end_phase()

    s1b = sb("s1b", [128, 2 * D], BF16, OFF_S1B)
    u_sb = sb("u_sb", [128, 2 * D], BF16, OFF_USB)
    s_ld0, s_ld1, s_mm, s_v, s_st0, s_st1 = sems("ld0", "ld1", "mm", "v", "st0", "st1")
    s_ld = [s_ld0, s_ld1]
    s_st = [s_st0, s_st1]
    with nc.Block() as blk:
        @blk.gpsimd
        def _(g):
            for bb in range(NHT):
                if bb >= 2:
                    g.wait_ge(s_mm.h, s_mm.base + (bb - 1))
                g.dma_start(s1b[:, 2048 * (bb % 2):2048 * (bb % 2 + 1)],
                            w2t[128 * bb:128 * (bb + 1), :]).then_inc(s_ld[bb % 2].h, s_ld[bb % 2].add(16))

        @blk.tensor
        def _(t):
            for bb in range(NHT):
                t.wait_ge(s_ld[bb % 2].h, s_ld[bb % 2].base + (16 * (bb // 2 + 1)))
                if bb >= 2:
                    t.wait_ge(s_v.h, s_v.base + (4 * (bb - 1)))
                for j in range(4):
                    ins = t.matmul(bank(4 * (bb % 2) + j), h128[:],
                                   s1b[:, 2048 * (bb % 2) + 512 * j:2048 * (bb % 2) + 512 * (j + 1)],
                                   start=True, stop=True)
                ins.then_inc(s_mm.h, s_mm.add(1))

        @blk.vector
        def _(v):
            for bb in range(NHT):
                v.wait_ge(s_mm.h, s_mm.base + (bb + 1))
                if bb >= 2:
                    v.wait_ge(s_st[bb % 2].h, s_st[bb % 2].base + (16 * ((bb - 2) // 2 + 1)))
                for j in range(4):
                    v.tensor_scalar_mul(
                        u_sb[:, 2048 * (bb % 2) + 512 * j:2048 * (bb % 2) + 512 * (j + 1)],
                        bank(4 * (bb % 2) + j), C_FWHT).then_inc(s_v.h, s_v.add(1))

        @blk.sync
        def _(s):
            for bb in range(NHT):
                s.wait_ge(s_v.h, s_v.base + (4 * (bb + 1)))
                s.dma_start(u_s[128 * bb:128 * (bb + 1), :],
                            u_sb[:, 2048 * (bb % 2):2048 * (bb % 2 + 1)]).then_inc(s_st[bb % 2].h, s_st[bb % 2].add(16))
            s.wait_ge(s_st0.h, s_st0.base + (16 * (NHT // 2)))
            s.wait_ge(s_st1.h, s_st1.base + (16 * (NHT // 2)))
    end_phase()

    if stages == "pb1":
        stages = "pbuild"
        skip_stage2 = True
    else:
        skip_stage2 = False
    v2b = sb("v2b", [64, 2 * D], BF16, OFF_V2B)
    p2sb = sb("p2sb", [64, 2 * D], BF16, OFF_P2SB)
    u_v = u_s.rearrange("(b p) d -> p b d", p=128)
    s_ld0, s_ld1, s_mm, s_v, s_st0, s_st1 = sems("ld2a", "ld2b", "mm", "v", "st2a", "st2b")
    s_ld = [s_ld0, s_ld1]
    s_st = [s_st0, s_st1]
    if skip_stage2:
        with nc.Block() as blk: pass
    with nc.Block() as blk:
        @blk.sync
        def _(s):
            for p in range(0 if skip_stage2 else 128):
                if p >= 2:
                    s.wait_ge(s_mm.h, s_mm.base + (p - 1))
                s.dma_start(v2b[:, 2048 * (p % 2):2048 * (p % 2 + 1)], u_v[p]).then_inc(s_ld[p % 2].h, s_ld[p % 2].add(16))

        @blk.tensor
        def _(t):
            for p in range(0 if skip_stage2 else 128):
                t.wait_ge(s_ld[p % 2].h, s_ld[p % 2].base + (16 * (p // 2 + 1)))
                if p >= 2:
                    t.wait_ge(s_v.h, s_v.base + (4 * (p - 1)))
                for j in range(4):
                    ins = t.matmul(bank(4 * (p % 2) + j)[0:64, :], h64[:],
                                   v2b[:, 2048 * (p % 2) + 512 * j:2048 * (p % 2) + 512 * (j + 1)],
                                   start=True, stop=True)
                ins.then_inc(s_mm.h, s_mm.add(1))

        @blk.vector
        def _(v):
            for p in range(0 if skip_stage2 else 128):
                v.wait_ge(s_mm.h, s_mm.base + (p + 1))
                if p >= 2:
                    v.wait_ge(s_st[p % 2].h, s_st[p % 2].base + (16 * ((p - 2) // 2 + 1)))
                for j in range(4):
                    v.tensor_copy(
                        p2sb[:, 2048 * (p % 2) + 512 * j:2048 * (p % 2) + 512 * (j + 1)],
                        bank(4 * (p % 2) + j)[0:64, :]).then_inc(s_v.h, s_v.add(1))

        @blk.scalar
        def _(sc):
            for p in range(0 if skip_stage2 else 128):
                sc.wait_ge(s_v.h, s_v.base + (4 * (p + 1)))
                sc.dma_start(p4[64 * p:64 * (p + 1), :],
                             p2sb[:, 2048 * (p % 2):2048 * (p % 2 + 1)]).then_inc(s_st[p % 2].h, s_st[p % 2].add(16))
            if not skip_stage2:
                sc.wait_ge(s_st0.h, s_st0.base + (16 * 64))
                sc.wait_ge(s_st1.h, s_st1.base + (16 * 64))
    end_phase()

    def emit_ln(src, sh):
        xTf = sb("xTf", [128, NDT * SH], F32, OFF_XTF)
        xb16 = sb("xb16", [128, NDT * SH], BF16, OFF_XB16)
        xsq = sb("xsq", [128, NDT * SH], BF16, OFF_XSQ)
        meanB = sb("meanB", [128, SH], F32, OFF_MEANB)
        rstdB = sb("rstdB", [128, SH], F32, OFF_RSTDB)
        stA = sb("stA", [1, SH], F32, OFF_STA)
        stB = sb("stB", [1, SH], F32, OFF_STB)
        stC = sb("stC", [1, SH], F32, OFF_STC)
        st16 = sb("st16", [1, 2 * SH], BF16, OFF_ST16)
        sxl = list(sems("sx0", "sx1", "sx2", "sx3"))
        sbl = list(sems("sb0", "sb1", "sb2", "sb3"))
        ssq, smm, svec, ssc2, svch = sems("ssq", "smm", "svec", "ssc2", "svch")
        col = SH

        with nc.Block() as blk:
            @blk.sync
            def _(s):
                for dt in range(NDT):
                    if dt >= 4:
                        s.wait_ge(ssq.h, ssq.base + (dt - 3))
                    s.dma_start(xTf[:, col * dt:col * (dt + 1)],
                                src[128 * dt:128 * (dt + 1), SH * sh:SH * (sh + 1)]).then_inc(sxl[dt % 4].h, sxl[dt % 4].add(16))

            @blk.gpsimd
            def _(g):
                for dt in range(NDT):
                    if dt >= 4:
                        g.wait_ge(smm.h, smm.base + (2 * (dt - 3)))
                    g.dma_start(xb16[:, col * dt:col * (dt + 1)],
                                src[128 * dt:128 * (dt + 1), SH * sh:SH * (sh + 1)]).then_inc(sbl[dt % 4].h, sbl[dt % 4].add(16))

            @blk.scalar
            def _(sc):
                for dt in range(NDT):
                    sc.wait_ge(sxl[dt % 4].h, sxl[dt % 4].base + (16 * (dt // 4 + 1)))
                    sc.activation(xsq[:, col * dt:col * (dt + 1)],
                                  xTf[:, col * dt:col * (dt + 1)], AF.Square).then_inc(ssq.h, ssq.add(1))
                sc.wait_ge(svec.h, svec.base + (1))
                sc.activation(stB[:], stB[:], AF.Sqrt).then_inc(ssc2.h, ssc2.add(1))

            @blk.tensor
            def _(t):
                for dt in range(NDT):
                    t.wait_ge(sbl[dt % 4].h, sbl[dt % 4].base + (16 * (dt // 4 + 1)))
                    for hf in range(2):
                        ins = t.matmul(bank(4 + hf)[0:1, :], ones_col[:],
                                       xb16[:, col * dt + 512 * hf:col * dt + 512 * (hf + 1)],
                                       start=(dt == 0), stop=(dt == NDT - 1), skip_group_check=True)
                    ins.then_inc(smm.h, smm.add(1))
                    t.wait_ge(ssq.h, ssq.base + (dt + 1))
                    for hf in range(2):
                        ins = t.matmul(bank(6 + hf)[0:1, :], ones_col[:],
                                       xsq[:, col * dt + 512 * hf:col * dt + 512 * (hf + 1)],
                                       start=(dt == 0), stop=(dt == NDT - 1), skip_group_check=True)
                    ins.then_inc(smm.h, smm.add(1))
                t.wait_ge(svec.h, svec.base + (2))
                for hf in range(2):
                    t.matmul(bank(4 + hf), ones_row[:],
                             st16[0:1, 512 * hf:512 * (hf + 1)], start=True, stop=True)
                for hf in range(2):
                    ins = t.matmul(bank(6 + hf), ones_row[:],
                                   st16[0:1, SH + 512 * hf:SH + 512 * (hf + 1)],
                                   start=True, stop=True)
                ins.then_inc(smm.h, smm.add(1))

            @blk.vector
            def _(v):
                ch = Chain(svch)
                v.wait_ge(smm.h, smm.base + (2 * NDT))
                inv = 1.0 / float(D)
                for hf in range(2):
                    ch.seal(v.tensor_scalar_mul(stA[0:1, 512 * hf:512 * (hf + 1)], bank(4 + hf)[0:1, :], inv))
                for hf in range(2):
                    ch.seal(v.tensor_scalar_mul(stB[0:1, 512 * hf:512 * (hf + 1)], bank(6 + hf)[0:1, :], inv))
                ch.wait(v)
                ch.seal(v.tensor_tensor(stC[:], stA[:], stA[:], op=ALU.mult))
                ch.wait(v)
                ch.seal(v.tensor_tensor(stB[:], stB[:], stC[:], op=ALU.subtract))
                ch.wait(v)
                v.tensor_scalar_add(stB[:], stB[:], EPS).then_inc(svec.h, svec.add(1))
                v.wait_ge(ssc2.h, ssc2.base + (1))
                ch.seal(v.reciprocal(stC[:], stB[:]))
                v.tensor_copy(st16[0:1, 0:SH], stA[:])
                ch.wait(v)
                v.tensor_copy(st16[0:1, SH:2 * SH], stC[:]).then_inc(svec.h, svec.add(1))
                v.wait_ge(smm.h, smm.base + (2 * NDT + 1))
                ch.seal(v.tensor_copy(meanB[:, :], pall[:, 4 * 512:6 * 512]))
                ch.seal(v.tensor_copy(rstdB[:, :], pall[:, 6 * 512:8 * 512]))
                ch.wait(v)
                for dt in range(NDT):
                    ch.seal(v.tensor_tensor(xTf[:, col * dt:col * (dt + 1)],
                                            xTf[:, col * dt:col * (dt + 1)], meanB[:], op=ALU.subtract))
                    ch.wait(v)
                    v.tensor_tensor(XNT[:, col * dt:col * (dt + 1)],
                                    xTf[:, col * dt:col * (dt + 1)], rstdB[:], op=ALU.mult)
        end_phase()

    def emit_up(sh):
        sw1 = list(sems("sw1a", "sw1b"))
        sw3 = list(sems("sw3a", "sw3b"))
        smg, sml, ssl, stt = sems("smg", "sml", "ssl", "stt")
        with nc.Block() as blk:
            @blk.gpsimd
            def _(g):
                for ht in range(NHT):
                    sl = 2048 * (ht % 2)
                    if ht >= 2:
                        g.wait_ge(smg.h, smg.base + (ht - 1))
                    g.dma_start(W1B[:, sl:sl + 2048], a1[ht]).then_inc(sw1[ht % 2].h, sw1[ht % 2].add(16))
                    if ht >= 2:
                        g.wait_ge(sml.h, sml.base + (ht - 1))
                    g.dma_start(W3B[:, sl:sl + 2048], a3[ht]).then_inc(sw3[ht % 2].h, sw3[ht % 2].add(16))

            @blk.tensor
            def _(t):
                for ht in range(NHT):
                    sl = 2048 * (ht % 2)
                    t.wait_ge(sw1[ht % 2].h, sw1[ht % 2].base + (16 * (ht // 2 + 1)))
                    if ht >= 1:
                        t.wait_ge(ssl.h, ssl.base + (2 * ht))
                    for dt in range(NDT):
                        for hf in range(2):
                            ins = t.matmul(bank(hf),
                                           W1B[:, sl + 128 * dt:sl + 128 * (dt + 1)],
                                           XNT[:, SH * dt + 512 * hf:SH * dt + 512 * (hf + 1)],
                                           start=(dt == 0), stop=(dt == NDT - 1),
                                           skip_group_check=True)
                    ins.then_inc(smg.h, smg.add(1))
                    t.wait_ge(sw3[ht % 2].h, sw3[ht % 2].base + (16 * (ht // 2 + 1)))
                    if ht >= 1:
                        t.wait_ge(stt.h, stt.base + (2 * ht))
                    for dt in range(NDT):
                        for hf in range(2):
                            ins = t.matmul(bank(2 + hf),
                                           W3B[:, sl + 128 * dt:sl + 128 * (dt + 1)],
                                           XNT[:, SH * dt + 512 * hf:SH * dt + 512 * (hf + 1)],
                                           start=(dt == 0), stop=(dt == NDT - 1),
                                           skip_group_check=True)
                    ins.then_inc(sml.h, sml.add(1))

            @blk.scalar
            def _(sc):
                for ht in range(NHT):
                    sc.wait_ge(smg.h, smg.base + (ht + 1))
                    if ht >= 2:
                        sc.wait_ge(stt.h, stt.base + (2 * (ht - 1)))
                    for hf in range(2):
                        sc.activation(
                            TMPS[:, 1024 * (ht % 2) + 512 * hf:1024 * (ht % 2) + 512 * (hf + 1)],
                            bank(hf), AF.Silu).then_inc(ssl.h, ssl.add(1))

            @blk.vector
            def _(v):
                for ht in range(NHT):
                    v.wait_ge(ssl.h, ssl.base + (2 * (ht + 1)))
                    v.wait_ge(sml.h, sml.base + (ht + 1))
                    for hf in range(2):
                        v.tensor_tensor(
                            HID[:, SH * ht + 512 * hf:SH * ht + 512 * (hf + 1)],
                            TMPS[:, 1024 * (ht % 2) + 512 * hf:1024 * (ht % 2) + 512 * (hf + 1)],
                            bank(2 + hf), op=ALU.mult).then_inc(stt.h, stt.add(1))
        end_phase()

    def emit_down(src, dst, sh):
        spl = list(sems("spa", "spb"))
        sxrl = list(sems("sxra", "sxrb"))
        sstl = list(sems("ssta", "sstb"))
        smd, svd = sems("smd", "svd")
        with nc.Block() as blk:
            @blk.sync
            def _(s):
                for dg in range(8):
                    for hb in range(8):
                        it = 8 * dg + hb
                        if it >= 2:
                            s.wait_ge(smd.h, smd.base + (it - 1))
                        srcv = p4[1024 * hb:1024 * (hb + 1), 256 * dg:256 * (dg + 1)] \
                            .rearrange("(h p) d -> p h d", p=128)
                        s.dma_start(
                            PBUF[:, 2048 * (it % 2):2048 * (it % 2 + 1)].rearrange(
                                "p (h d) -> p h d", h=8), srcv).then_inc(spl[it % 2].h, spl[it % 2].add(16))

            @blk.scalar
            def _(sc):
                for dg in range(8):
                    if dg >= 2:
                        sc.wait_ge(svd.h, svd.base + (4 * (dg - 1)))
                    for tt in range(4):
                        dc, hf = tt // 2, tt % 2
                        dch = 2 * dg + dc
                        sc.dma_start(
                            XRES[:, 512 * (4 * (dg % 2) + tt):512 * (4 * (dg % 2) + tt + 1)],
                            src[128 * dch:128 * (dch + 1),
                                SH * sh + 512 * hf:SH * sh + 512 * (hf + 1)]).then_inc(sxrl[dg % 2].h, sxrl[dg % 2].add(16))
                    if dg >= 1:
                        for tt in range(4):
                            dc, hf = tt // 2, tt % 2
                            dch = 2 * (dg - 1) + dc
                            sc.wait_ge(svd.h, svd.base + (4 * (dg - 1) + tt + 1))
                            sc.dma_start(
                                dst[128 * dch:128 * (dch + 1),
                                    SH * sh + 512 * hf:SH * sh + 512 * (hf + 1)],
                                XNEW[:, 512 * (4 * ((dg - 1) % 2) + tt):512 * (4 * ((dg - 1) % 2) + tt + 1)]
                            ).then_inc(sstl[(dg - 1) % 2].h, sstl[(dg - 1) % 2].add(16))
                for tt in range(4):
                    dc, hf = tt // 2, tt % 2
                    dch = 14 + dc
                    sc.wait_ge(svd.h, svd.base + (28 + tt + 1))
                    sc.dma_start(
                        dst[128 * dch:128 * (dch + 1),
                            SH * sh + 512 * hf:SH * sh + 512 * (hf + 1)],
                        XNEW[:, 512 * (4 + tt):512 * (4 + tt + 1)]).then_inc(sstl[1].h, sstl[1].add(16))
                sc.wait_ge(sstl[0].h, sstl[0].base + (16 * 16))
                sc.wait_ge(sstl[1].h, sstl[1].base + (16 * 16))

            @blk.tensor
            def _(t):
                for dg in range(8):
                    if dg >= 2:
                        t.wait_ge(svd.h, svd.base + (4 * (dg - 1)))
                    for hb in range(8):
                        it = 8 * dg + hb
                        t.wait_ge(spl[it % 2].h, spl[it % 2].base + (16 * (it // 2 + 1)))
                        for hl in range(8):
                            ht = 8 * hb + hl
                            for tt in range(4):
                                dc, hf = tt // 2, tt % 2
                                ins = t.matmul(
                                    bank(4 * (dg % 2) + tt),
                                    PBUF[:, 2048 * (it % 2) + 256 * hl + 128 * dc:
                                         2048 * (it % 2) + 256 * hl + 128 * (dc + 1)],
                                    HID[:, SH * ht + 512 * hf:SH * ht + 512 * (hf + 1)],
                                    start=(ht == 0), stop=(ht == NHT - 1),
                                    skip_group_check=True)
                        ins.then_inc(smd.h, smd.add(1))

            @blk.vector
            def _(v):
                for dg in range(8):
                    v.wait_ge(smd.h, smd.base + (8 * dg + 8))
                    v.wait_ge(sxrl[dg % 2].h, sxrl[dg % 2].base + (64 * (dg // 2 + 1)))
                    if dg >= 2:
                        v.wait_ge(sstl[dg % 2].h, sstl[dg % 2].base + (64 * ((dg - 2) // 2 + 1)))
                    for tt in range(4):
                        v.tensor_tensor(
                            XNEW[:, 512 * (4 * (dg % 2) + tt):512 * (4 * (dg % 2) + tt + 1)],
                            XRES[:, 512 * (4 * (dg % 2) + tt):512 * (4 * (dg % 2) + tt + 1)],
                            bank(4 * (dg % 2) + tt), op=ALU.add).then_inc(svd.h, svd.add(1))
        end_phase()

    def emit_ln_stream(src, sh):
        stA = sb("stA", [1, SH], F32, OFF_W1B)
        stB = sb("stB", [1, SH], F32, OFF_W1B + 4 * KB)
        stC = sb("stC", [1, SH], F32, OFF_W1B + 8 * KB)
        st16 = sb("st16", [1, 2 * SH], BF16, OFF_W1B + 12 * KB)
        bb16 = sb("bb16", [128, 2 * SH], BF16, OFF_W1B + 16 * KB)
        bsq = sb("bsq", [128, 2 * SH], BF16, OFF_W1B + 20 * KB)
        meanB = sb("meanB", [128, SH], F32, OFF_W1B + 16 * KB)
        rstdB = sb("rstdB", [128, SH], F32, OFF_W1B + 20 * KB)
        xTfb = sb("xTfb", [128, 2 * SH], F32, OFF_W1B)

        sbl2 = list(sems("sbsa", "sbsb"))
        sxfl = list(sems("sxfa", "sxfb"))
        ssq, smm, svec, ssc2, sap, svch2 = sems("ssq2", "smm2", "svec2", "ssc22", "sap", "svch2")
        with nc.Block() as blk:
            @blk.gpsimd
            def _(g):
                for dt in range(NDT):
                    if dt >= 2:
                        g.wait_ge(smm.h, smm.base + 2 * (dt - 1))
                    g.dma_start(bb16[:, SH * (dt % 2):SH * (dt % 2 + 1)],
                                src[128 * dt:128 * (dt + 1), SH * sh:SH * (sh + 1)]).then_inc(sbl2[dt % 2].h, sbl2[dt % 2].add(16))

            @blk.scalar
            def _(sc):
                for dt in range(NDT):
                    sc.wait_ge(sbl2[dt % 2].h, sbl2[dt % 2].base + (16 * (dt // 2 + 1)))
                    if dt >= 2:
                        sc.wait_ge(smm.h, smm.base + (2 * (dt - 1)))
                    sc.activation(bsq[:, SH * (dt % 2):SH * (dt % 2 + 1)],
                                  bb16[:, SH * (dt % 2):SH * (dt % 2 + 1)], AF.Square).then_inc(ssq.h, ssq.add(1))
                sc.wait_ge(svec.h, svec.base + (1))
                sc.activation(stB[:], stB[:], AF.Sqrt).then_inc(ssc2.h, ssc2.add(1))

            @blk.tensor
            def _(t):
                for dt in range(NDT):
                    t.wait_ge(ssq.h, ssq.base + (dt + 1))
                    for hf in range(2):
                        ins = t.matmul(bank(4 + hf)[0:1, :], ones_col[:],
                                       bb16[:, SH * (dt % 2) + 512 * hf:SH * (dt % 2) + 512 * (hf + 1)],
                                       start=(dt == 0), stop=(dt == NDT - 1), skip_group_check=True)
                    ins.then_inc(smm.h, smm.add(1))
                    for hf in range(2):
                        ins = t.matmul(bank(6 + hf)[0:1, :], ones_col[:],
                                       bsq[:, SH * (dt % 2) + 512 * hf:SH * (dt % 2) + 512 * (hf + 1)],
                                       start=(dt == 0), stop=(dt == NDT - 1), skip_group_check=True)
                    ins.then_inc(smm.h, smm.add(1))
                t.wait_ge(svec.h, svec.base + (2))
                for hf in range(2):
                    t.matmul(bank(4 + hf), ones_row[:],
                             st16[0:1, 512 * hf:512 * (hf + 1)], start=True, stop=True)
                for hf in range(2):
                    ins = t.matmul(bank(6 + hf), ones_row[:],
                                   st16[0:1, SH + 512 * hf:SH + 512 * (hf + 1)],
                                   start=True, stop=True)
                ins.then_inc(smm.h, smm.add(1))

            @blk.vector
            def _(v):
                ch = Chain(svch2)
                v.wait_ge(smm.h, smm.base + (2 * NDT))
                inv = 1.0 / float(D)
                for hf in range(2):
                    ch.seal(v.tensor_scalar_mul(stA[0:1, 512 * hf:512 * (hf + 1)], bank(4 + hf)[0:1, :], inv))
                for hf in range(2):
                    ch.seal(v.tensor_scalar_mul(stB[0:1, 512 * hf:512 * (hf + 1)], bank(6 + hf)[0:1, :], inv))
                ch.wait(v)
                ch.seal(v.tensor_tensor(stC[:], stA[:], stA[:], op=ALU.mult))
                ch.wait(v)
                ch.seal(v.tensor_tensor(stB[:], stB[:], stC[:], op=ALU.subtract))
                ch.wait(v)
                v.tensor_scalar_add(stB[:], stB[:], EPS).then_inc(svec.h, svec.add(1))
                v.wait_ge(ssc2.h, ssc2.base + (1))
                ch.seal(v.reciprocal(stC[:], stB[:]))
                v.tensor_copy(st16[0:1, 0:SH], stA[:])
                ch.wait(v)
                v.tensor_copy(st16[0:1, SH:2 * SH], stC[:]).then_inc(svec.h, svec.add(1))
                v.wait_ge(smm.h, smm.base + (2 * NDT + 1))
                ch.seal(v.tensor_copy(meanB[:, :], pall[:, 4 * 512:6 * 512]))
                ch.seal(v.tensor_copy(rstdB[:, :], pall[:, 6 * 512:8 * 512]))
                ch.wait(v)
                v.engine_nop().then_inc(svec.h, svec.add(1))
                for dt in range(NDT):
                    v.wait_ge(sxfl[dt % 2].h, sxfl[dt % 2].base + (16 * (dt // 2 + 1)))
                    ch.seal(v.tensor_tensor(xTfb[:, SH * (dt % 2):SH * (dt % 2 + 1)],
                                            xTfb[:, SH * (dt % 2):SH * (dt % 2 + 1)], meanB[:], op=ALU.subtract))
                    ch.wait(v)
                    v.tensor_tensor(XNT[:, SH * dt:SH * (dt + 1)],
                                    xTfb[:, SH * (dt % 2):SH * (dt % 2 + 1)], rstdB[:],
                                    op=ALU.mult).then_inc(sap.h, sap.add(1))

            @blk.sync
            def _(s):
                s.wait_ge(svec.h, svec.base + 3)
                for dt in range(NDT):
                    if dt >= 2:
                        s.wait_ge(sap.h, sap.base + (dt - 1))
                    s.dma_start(xTfb[:, SH * (dt % 2):SH * (dt % 2 + 1)],
                                src[128 * dt:128 * (dt + 1), SH * sh:SH * (sh + 1)]).then_inc(sxfl[dt % 2].h, sxfl[dt % 2].add(16))
        end_phase()

    def emit_rope_consts():
        (s_c2,) = sems("c2")
        with nc.Block() as blk:
            @blk.sync
            def _(s):
                s.dma_start(COSS[:], cosT_in).then_inc(s_c2.h, s_c2.add(16))
                s.dma_start(SINSS[:], sinS_in).then_inc(s_c2.h, s_c2.add(16))
                s.wait_ge(s_c2.h, s_c2.base + (32))
        end_phase()

    def emit_qkv(sh):
        QRAW = sb("qraw", [128, 2 * 512], BF16, OFF_QRAW)
        QP = sb("qp", [128, 2 * 512], BF16, OFF_QP)
        T1 = sb("t1", [128, 2 * 512], F32, OFF_T1)
        T2 = sb("t2", [128, 2 * 512], F32, OFF_T2)

        tiles = [("q", ct, hf) for ct in range(16) for hf in range(2)] + \
                [("k", ct, hf) for ct in range(4) for hf in range(2)]

        swql = list(sems("swqa", "swqb"))
        sqsl = list(sems("sqsa", "sqsb"))
        smq, smr, sv1, sv2, sv3, smv, svv, swv, svch3 = sems(
            "smq", "smr", "sv1", "sv2", "sv3", "smv", "svv", "swv", "svch3")

        with nc.Block() as blk:
            @blk.gpsimd
            def _(g):
                for ci in range(20):
                    if ci >= 2:
                        g.wait_ge(smq.h, smq.base + (2 * (ci - 1)))
                    srcw = aq[ci] if ci < 16 else ak[ci - 16]
                    g.dma_start(W1B[:, 2048 * (ci % 2):2048 * (ci % 2 + 1)], srcw).then_inc(swql[ci % 2].h, swql[ci % 2].add(16))
                for dt in range(NDT):
                    g.dma_start(WVB[:, 512 * dt:512 * (dt + 1)], av[dt]).then_inc(swv.h, swv.add(16))

            @blk.tensor
            def _(t):
                for i, (kind, ct, hf) in enumerate(tiles):
                    ci = i // 2
                    par = i % 2
                    t.wait_ge(swql[ci % 2].h, swql[ci % 2].base + (16 * (ci // 2 + 1)))
                    if i >= 2:
                        t.wait_ge(sv1.h, sv1.base + (i - 1))
                    for dt in range(NDT):
                        ins = t.matmul(bank(par),
                                       W1B[:, 2048 * (ci % 2) + 128 * dt:2048 * (ci % 2) + 128 * (dt + 1)],
                                       XNT[:, SH * dt + 512 * hf:SH * dt + 512 * (hf + 1)],
                                       start=(dt == 0), stop=(dt == NDT - 1),
                                       skip_group_check=True)
                    ins.then_inc(smq.h, smq.add(1))
                    t.wait_ge(sv1.h, sv1.base + (i + 1))
                    if i >= 2:
                        t.wait_ge(sv2.h, sv2.base + (i - 1))
                    t.matmul(bank(2 + par), rotm[:],
                             QRAW[:, 512 * par:512 * (par + 1)],
                             start=True, stop=True).then_inc(smr.h, smr.add(1))
                t.wait_ge(swv.h, swv.base + (16 * NDT))
                for st in range(8):
                    if st >= 2:
                        t.wait_ge(svv.h, svv.base + (st - 1))
                    for dt in range(NDT):
                        ins = t.matmul(bank(4 + st % 2),
                                       XNT[:, SH * dt + 128 * st:SH * dt + 128 * (st + 1)],
                                       WVB[:, 512 * dt:512 * (dt + 1)],
                                       start=(dt == 0), stop=(dt == NDT - 1),
                                       skip_group_check=True)
                    ins.then_inc(smv.h, smv.add(1))

            @blk.vector
            def _(v):
                ch = Chain(svch3)
                for i, (kind, ct, hf) in enumerate(tiles):
                    par = i % 2
                    so = SH * sh + 512 * hf
                    v.wait_ge(smq.h, smq.base + (i + 1))
                    if i >= 2:
                        v.wait_ge(smr.h, smr.base + (i - 1))
                    ch.seal(v.tensor_copy(QRAW[:, 512 * par:512 * (par + 1)], bank(par)))
                    ch.wait(v)
                    v.engine_nop().then_inc(sv1.h, sv1.add(1))
                    ch.seal(v.tensor_tensor(T1[:, 512 * par:512 * (par + 1)],
                                            QRAW[:, 512 * par:512 * (par + 1)],
                                            COSS[:, so:so + 512], op=ALU.mult))
                    v.wait_ge(smr.h, smr.base + (i + 1))
                    ch.seal(v.tensor_tensor(T2[:, 512 * par:512 * (par + 1)], bank(2 + par),
                                            SINSS[:, so:so + 512], op=ALU.mult))
                    ch.wait(v)
                    v.engine_nop().then_inc(sv2.h, sv2.add(1))
                    if kind == "q":
                        if i >= 2:
                            v.wait_ge(sqsl[i % 2].h, sqsl[i % 2].base + (16 * ((i - 2) // 2 + 1)))
                        v.tensor_tensor(QP[:, 512 * par:512 * (par + 1)],
                                        T1[:, 512 * par:512 * (par + 1)],
                                        T2[:, 512 * par:512 * (par + 1)], op=ALU.add).then_inc(sv3.h, sv3.add(1))
                    else:
                        v.tensor_tensor(KTT[:, S * ct + so:S * ct + so + 512],
                                        T1[:, 512 * par:512 * (par + 1)],
                                        T2[:, 512 * par:512 * (par + 1)], op=ALU.add).then_inc(sv3.h, sv3.add(1))
                for st in range(8):
                    stg = 8 * sh + st
                    v.wait_ge(smv.h, smv.base + (st + 1))
                    ch.seal(v.memset(VAUG[:, 520 * stg:520 * (stg + 1)], 1.0))
                    ch.wait(v)
                    v.tensor_copy(
                        VAUG[:, 520 * stg:520 * (stg + 1)].rearrange(
                            "p (g k) -> p g k", g=8, k=65)[:, :, 0:64],
                        bank(4 + st % 2).rearrange("p (g k) -> p g k", g=8),
                    ).then_inc(svv.h, svv.add(1))

            @blk.sync
            def _(s):
                for i, (kind, ct, hf) in enumerate(tiles):
                    if kind != "q":
                        continue
                    par = i % 2
                    s.wait_ge(sv3.h, sv3.base + (i + 1))
                    s.dma_start(qs[ct][:, SH * sh + 512 * hf:SH * sh + 512 * (hf + 1)],
                                QP[:, 512 * par:512 * (par + 1)]).then_inc(sqsl[i % 2].h, sqsl[i % 2].add(16))
                s.wait_ge(sqsl[0].h, sqsl[0].base + (16 * 16))
                s.wait_ge(sqsl[1].h, sqsl[1].base + (16 * 16))
        end_phase()

    def emit_scores():
        QBUF = sb("qbuf", [128, 2 * 512], BF16, OFF_QBUF)
        PRB = sb("prb", [128, 4 * 512], BF16, OFF_PRB)
        DENS = sb("dens", [128, 2 * 1024], F32, OFF_T1)

        sqll = list(sems("sqla", "sqlb"))
        sdsl = list(sems("sdsa", "sdsb"))
        sms, sse, smk, smav, sout, sdnv = sems(
            "sms", "sse", "smk", "smav", "sout", "sdnv")

        its = [(c, dkt, j) for c in range(4) for dkt in range(4) for j in range(4)]
        NIT = len(its)

        sms_h, sse_h, smk_h = {}, {}, {}
        av_cum = [0] * (NIT + 1)
        c_sms = c_sse = c_smk = 0
        for idx, (c, dkt, j) in enumerate(its):
            nkt = 4 * c + 4
            for kt in range(nkt):
                c_sms += 1
                sms_h[(idx, kt)] = c_sms
                c_sse += 1
                sse_h[(idx, kt)] = c_sse
                if kt >= 4 * c:
                    c_smk += 1
                smk_h[(idx, kt)] = c_smk
            av_cum[idx + 1] = av_cum[idx] + 2 * nkt
        sms_cum = {idx: sms_h[(idx, 4 * its[idx][0] + 3)] for idx in range(NIT)}

        def prb_ap(par, off):
            return PRB[:, 512 * (2 * par):512 * (2 * par) + 1024].rearrange(
                "p (b k) -> p b k", b=2)[:, :, off:512]

        def sco_ap(par, off):
            return pall[:, 512 * (2 * par):512 * (2 * par) + 1024].rearrange(
                "p (b k) -> p b k", b=2)[:, :, off:512]

        with nc.Block() as blk:
            @blk.sync
            def _(s):
                for idx, (c, dkt, j) in enumerate(its):
                    hA, hB = 8 * dkt + j, 8 * dkt + 4 + j
                    if idx >= 2:
                        s.wait_ge(sms.h, sms.base + (sms_cum[idx - 2]))
                    qb = 512 * (idx % 2)
                    for which, h in ((0, hA), (1, hB)):
                        s.dma_start(QBUF[64 * which:64 * (which + 1), qb:qb + 512],
                                    qs[h // 2][64 * (h % 2):64 * (h % 2) + 64, 512 * c:512 * (c + 1)]
                                    ).then_inc(sqll[idx % 2].h, sqll[idx % 2].add(16))
                    if idx >= 1:
                        s.wait_ge(sdnv.h, sdnv.base + (idx))
                        jp = (idx - 1) % 2
                        for w in range(2):
                            s.dma_start(den_s[2 * (idx - 1) + w:2 * (idx - 1) + w + 1, :],
                                        DENS[64:65, 1024 * jp + 512 * w:1024 * jp + 512 * (w + 1)]
                                        ).then_inc(sdsl[jp].h, sdsl[jp].add(16))
                s.wait_ge(sdnv.h, sdnv.base + (NIT))
                jp = (NIT - 1) % 2
                for w in range(2):
                    s.dma_start(den_s[2 * (NIT - 1) + w:2 * (NIT - 1) + w + 1, :],
                                DENS[64:65, 1024 * jp + 512 * w:1024 * jp + 512 * (w + 1)]
                                ).then_inc(sdsl[jp].h, sdsl[jp].add(16))
                s.wait_ge(sdsl[0].h, sdsl[0].base + (16 * 64))
                s.wait_ge(sdsl[1].h, sdsl[1].base + (16 * 64))

            @blk.tensor
            def _(t):
                for idx, (c, dkt, j) in enumerate(its):
                    nkt = 4 * c + 4
                    gA, gB = (8 * dkt + j) // 4, (8 * dkt + 4 + j) // 4
                    ip = idx % 2
                    qb = 512 * ip
                    t.wait_ge(sqll[ip].h, sqll[ip].base + (32 * (idx // 2 + 1)))
                    if idx >= 2:
                        t.wait_ge(sout.h, sout.base + (2 * (idx - 1)))
                    def score_mm(kt):
                        off = 128 * (kt - 4 * c) if kt >= 4 * c else 0
                        par = kt % 2
                        if kt >= 2:
                            t.wait_ge(sse.h, sse.base + (sse_h[(idx, kt - 2)]))
                        elif idx >= 1:
                            t.wait_ge(sse.h, sse.base + (sse_h[(idx - 1, 4 * its[idx - 1][0] + 3)]))
                        t.matmul(bank(2 * par)[:, off:512],
                                 KTT[0:64, S * dkt + 128 * kt:S * dkt + 128 * (kt + 1)],
                                 QBUF[0:64, qb + off:qb + 512],
                                 start=True, stop=True, tile_position=(0, 0))
                        ins = t.matmul(bank(2 * par + 1)[:, off:512],
                                       KTT[64:128, S * dkt + 128 * kt:S * dkt + 128 * (kt + 1)],
                                       QBUF[64:128, qb + off:qb + 512],
                                       start=True, stop=True, tile_position=(64, 0))
                        ins.then_inc(sms.h, sms.add(1))

                    score_mm(0)
                    for kt in range(nkt):
                        off = 128 * (kt - 4 * c) if kt >= 4 * c else 0
                        par = kt % 2
                        if kt + 1 < nkt:
                            score_mm(kt + 1)
                        t.wait_ge(sse.h, sse.base + (sse_h[(idx, kt)]))
                        if kt >= 4 * c:
                            t.wait_ge(smk.h, smk.base + (smk_h[(idx, kt)]))
                        t.matmul(bank(4 + 2 * ip)[0:65, off:512],
                                 VAUG[:, 520 * kt + 65 * gA:520 * kt + 65 * gA + 65],
                                 PRB[:, 512 * (2 * par) + off:512 * (2 * par) + 512],
                                 start=(kt == 0), stop=(kt == nkt - 1),
                                 skip_group_check=True).then_inc(smav.h, smav.add(1))
                        t.matmul(bank(5 + 2 * ip)[0:65, off:512],
                                 VAUG[:, 520 * kt + 65 * gB:520 * kt + 65 * gB + 65],
                                 PRB[:, 512 * (2 * par + 1) + off:512 * (2 * par + 1) + 512],
                                 start=(kt == 0), stop=(kt == nkt - 1),
                                 skip_group_check=True).then_inc(smav.h, smav.add(1))

            @blk.scalar
            def _(sc):
                for idx, (c, dkt, j) in enumerate(its):
                    nkt = 4 * c + 4
                    for kt in range(nkt):
                        off = 128 * (kt - 4 * c) if kt >= 4 * c else 0
                        par = kt % 2
                        sc.wait_ge(sms.h, sms.base + (sms_h[(idx, kt)]))
                        if kt >= 2:
                            sc.wait_ge(smav.h, smav.base + (av_cum[idx] + 2 * (kt - 1)))
                        elif idx >= 1:
                            sc.wait_ge(smav.h, smav.base + (av_cum[idx]))
                        sc.activation(prb_ap(par, off), sco_ap(par, off),
                                      AF.Exp, scale=0.125).then_inc(sse.h, sse.add(1))

            @blk.vector
            def _(v):
                trib = tril[:].rearrange("p (o k) -> p o k", o=1).broadcast_to((128, 2, 128))
                for idx, (c, dkt, j) in enumerate(its):
                    hA, hB = 8 * dkt + j, 8 * dkt + 4 + j
                    nkt = 4 * c + 4
                    ip = idx % 2
                    for kt in range(4 * c, nkt):
                        off = 128 * (kt - 4 * c)
                        par = kt % 2
                        v.wait_ge(sse.h, sse.base + (sse_h[(idx, kt)]))
                        v.tensor_tensor(prb_ap(par, off)[:, :, 0:128],
                                        prb_ap(par, off)[:, :, 0:128],
                                        trib, op=ALU.mult).then_inc(smk.h, smk.add(1))
                    v.wait_ge(smav.h, smav.base + (av_cum[idx + 1]))
                    if idx >= 2:
                        v.wait_ge(sdsl[ip].h, sdsl[ip].base + (32 * ((idx - 2) // 2 + 1)))
                    for which, h in ((0, hA), (1, hB)):
                        v.tensor_copy(
                            OUTT[64 * (h % 2):64 * (h % 2) + 64,
                                 S * (h // 2) + 512 * c:S * (h // 2) + 512 * (c + 1)],
                            bank(4 + which + 2 * ip)[0:64, :]).then_inc(sout.h, sout.add(1))
                    v.tensor_copy(
                        DENS[64:65, 1024 * ip:1024 * ip + 1024].rearrange("p (b k) -> p b k", b=2),
                        pall[:, 512 * (4 + 2 * ip):512 * (6 + 2 * ip)].rearrange(
                            "p (b k) -> p b k", b=2)[64:65, :, :],
                    ).then_inc(sdnv.h, sdnv.add(1))
        end_phase()

    def emit_norm():
        DENB = sb("denb", [128, 512], F32, OFF_T1)
        RCA = sb("rca", [128, 512], F32, OFF_T1 + 2 * KB)
        RCB = sb("rcb", [128, 4 * 512], F32, OFF_TMPS)

        its = [(c, dkt, j) for c in range(4) for dkt in range(4) for j in range(4)]
        sdl, svr, srw, svt = sems("sdl", "svr", "srw", "svt")
        srbl = list(sems("srb0", "srb1", "srb2", "srb3"))

        with nc.Block() as blk:
            @blk.sync
            def _(s):
                s.dma_start(DENB[:], den_s[:, :]).then_inc(sdl.h, sdl.add(16))
                s.wait_ge(svr.h, svr.base + (1))
                s.dma_start(rc_s[:, :], RCA[:]).then_inc(srw.h, srw.add(16))
                s.wait_ge(srw.h, srw.base + (16))
                for r in range(128):
                    if r >= 4:
                        s.wait_ge(svt.h, svt.base + (r - 3))
                    s.dma_start(RCB[:, 512 * (r % 4):512 * (r % 4 + 1)],
                                rc_s[r:r + 1, :].broadcast_to((128, 512))
                                ).then_inc(srbl[r % 4].h, srbl[r % 4].add(16))

            @blk.vector
            def _(v):
                v.wait_ge(sdl.h, sdl.base + (16))
                v.reciprocal(RCA[:], DENB[:]).then_inc(svr.h, svr.add(1))
                for r in range(128):
                    idx, which = r // 2, r % 2
                    c, dkt, j = its[idx]
                    h = 8 * dkt + j + 4 * which
                    v.wait_ge(srbl[r % 4].h, srbl[r % 4].base + (16 * (r // 4 + 1)))
                    hp = 64 * (h % 2)
                    sl = OUTT[hp:hp + 64,
                              S * (h // 2) + 512 * c:S * (h // 2) + 512 * (c + 1)]
                    v.tensor_tensor(sl, sl, RCB[hp:hp + 64, 512 * (r % 4):512 * (r % 4 + 1)],
                                    op=ALU.mult).then_inc(svt.h, svt.add(1))
        end_phase()

    def emit_outproj(src, dst):
        swol = list(sems("swoa", "swob"))
        sxr2l = list(sems("sxr2a", "sxr2b"))
        sst2l = list(sems("sst2a", "sst2b"))
        smo, svo = sems("smo", "svo")
        with nc.Block() as blk:
            @blk.gpsimd
            def _(g):
                for dch in range(NDT):
                    if dch >= 2:
                        g.wait_ge(smo.h, smo.base + (dch - 1))
                    g.dma_start(W3B[:, 2048 * (dch % 2):2048 * (dch % 2 + 1)],
                                ao[dch]).then_inc(swol[dch % 2].h, swol[dch % 2].add(16))

            @blk.scalar
            def _(sc):
                for dch in range(NDT):
                    if dch >= 2:
                        sc.wait_ge(svo.h, svo.base + (4 * (dch - 1)))
                    for hf in range(4):
                        sc.dma_start(
                            XRES[:, 512 * (4 * (dch % 2) + hf):512 * (4 * (dch % 2) + hf + 1)],
                            src[128 * dch:128 * (dch + 1), 512 * hf:512 * (hf + 1)]).then_inc(sxr2l[dch % 2].h, sxr2l[dch % 2].add(16))
                    if dch >= 1:
                        for hf in range(4):
                            sc.wait_ge(svo.h, svo.base + (4 * (dch - 1) + hf + 1))
                            sc.dma_start(
                                dst[128 * (dch - 1):128 * dch, 512 * hf:512 * (hf + 1)],
                                XNEW[:, 512 * (4 * ((dch - 1) % 2) + hf):512 * (4 * ((dch - 1) % 2) + hf + 1)]
                            ).then_inc(sst2l[(dch - 1) % 2].h, sst2l[(dch - 1) % 2].add(16))
                for hf in range(4):
                    sc.wait_ge(svo.h, svo.base + (60 + hf + 1))
                    sc.dma_start(dst[128 * 15:2048, 512 * hf:512 * (hf + 1)],
                                 XNEW[:, 512 * (4 + hf):512 * (4 + hf + 1)]).then_inc(sst2l[1].h, sst2l[1].add(16))
                sc.wait_ge(sst2l[0].h, sst2l[0].base + (16 * 32))
                sc.wait_ge(sst2l[1].h, sst2l[1].base + (16 * 32))

            @blk.tensor
            def _(t):
                for dch in range(NDT):
                    t.wait_ge(swol[dch % 2].h, swol[dch % 2].base + (16 * (dch // 2 + 1)))
                    if dch >= 2:
                        t.wait_ge(svo.h, svo.base + (4 * (dch - 1)))
                    for dvt in range(NDT):
                        for hf in range(4):
                            ins = t.matmul(
                                bank(4 * (dch % 2) + hf),
                                W3B[:, 2048 * (dch % 2) + 128 * dvt:2048 * (dch % 2) + 128 * (dvt + 1)],
                                OUTT[:, S * dvt + 512 * hf:S * dvt + 512 * (hf + 1)],
                                start=(dvt == 0), stop=(dvt == NDT - 1),
                                skip_group_check=True)
                    ins.then_inc(smo.h, smo.add(1))

            @blk.vector
            def _(v):
                for dch in range(NDT):
                    v.wait_ge(smo.h, smo.base + (dch + 1))
                    v.wait_ge(sxr2l[dch % 2].h, sxr2l[dch % 2].base + (64 * (dch // 2 + 1)))
                    if dch >= 2:
                        v.wait_ge(sst2l[dch % 2].h, sst2l[dch % 2].base + (64 * ((dch - 2) // 2 + 1)))
                    for hf in range(4):
                        v.tensor_tensor(
                            XNEW[:, 512 * (4 * (dch % 2) + hf):512 * (4 * (dch % 2) + hf + 1)],
                            XRES[:, 512 * (4 * (dch % 2) + hf):512 * (4 * (dch % 2) + hf + 1)],
                            bank(4 * (dch % 2) + hf), op=ALU.add).then_inc(svo.h, svo.add(1))
        end_phase()

    if stages != "pbuild":
        for sh in range(2):
            emit_ln(xin, sh)
            emit_up(sh)
            emit_down(xin, xa, sh)
        if stages != "ffn1":
            emit_rope_consts()
            for sh in range(2):
                emit_ln_stream(xa, sh)
                emit_qkv(sh)
            emit_scores()
            emit_norm()
            emit_outproj(xa, xb)
            if stages != "attn":
                for sh in range(2):
                    emit_ln(xb, sh)
                    emit_up(sh)
                    emit_down(xb, yT, sh)

    if debug_taps:
        (s_dt,) = sems("dt")
        taps = {"p4": (p4, [HF, D], BF16), "u_s": (u_s, [HF, D], BF16),
                "xa": (xa, [D, S], F32), "xb": (xb, [D, S], F32),
                "qs": (qs, [16, 128, S], BF16),
                "outT": (OUTT[:], [128, NDT * S], BF16),
                "kT": (KTT[:], [128, 4 * S], BF16),
                "vaug": (VAUG[:], [128, 16 * 520], BF16)}
        with nc.Block() as blk:
            @blk.sync
            def _(s):
                n = 0
                for name in debug_taps:
                    src_ap, shp, dt_ = taps[name]
                    dstp = nc.dram_tensor(f"tap_{name}", shp, dt_, kind="ExternalOutput").ap()
                    s.dma_start(dstp, src_ap).then_inc(s_dt.h, s_dt.add(16))
                    n += 1
                s.wait_ge(s_dt.h, s_dt.base + (16 * n))
        end_phase()

    nc.finalize()
    return nc


def make_in_maps(inputs):
    hs = np.asarray(inputs["L0_attn_hidden_states"])
    w = host_prepare(inputs)
    in_maps = []
    for b in range(B):
        m = {"xin": np.ascontiguousarray(hs[b].T)}
        for k in ("a1", "a3", "w2t", "aq", "ak", "av", "ao",
                  "h128", "h64", "cosT", "sinS", "rotm", "tril"):
            m[k] = w[k]
        in_maps.append(m)
    return in_maps


def kernel(**inputs):
    nc = build_nc()
    in_maps = make_in_maps(inputs)
    res = run_bass_kernel_spmd(nc, in_maps, core_ids=list(range(B)))
    out = np.empty((B, S, D), np.float32)
    for b in range(B):
        out[b] = res.results[b]["yT"].T
    return out


# version 05
# speedup vs baseline: 1.1276x; 1.1276x over previous
import numpy as np
import ml_dtypes
import concourse.bass as bass
import concourse.mybir as mybir
from concourse.bass_utils import run_bass_kernel_spmd

F32 = mybir.dt.float32
BF16 = mybir.dt.bfloat16
AF = mybir.ActivationFunctionType
ALU = mybir.AluOpType

B, S, D, HF = 8, 2048, 2048, 8192
NDT = D // 128
NHT = HF // 128
SH = S // 2
HD = 64
THETA = 10000.0
EPS = 1e-5
C_FWHT = 0.5 / float(np.sqrt(HF))
BF = ml_dtypes.bfloat16
KB = 1024

OFF_HID = 0
OFF_XTF = 0
OFF_XB16 = 64 * KB
OFF_XSQ = 96 * KB
OFF_MEANB = 128 * KB
OFF_RSTDB = 132 * KB
OFF_S1B = 0
OFF_USB = 8 * KB
OFF_V2B = 16 * KB
OFF_P2SB = 24 * KB
OFF_COSS = 0
OFF_SINS = 8 * KB
OFF_KT = 16 * KB
OFF_VAUG = 32 * KB
OFF_WVB = 49 * KB
OFF_QBUF = 49 * KB
OFF_PRB = 51 * KB
OFF_AVS = 55 * KB
OFF_RC = 59 * KB
OFF_RC16 = 63 * KB
OFF_OUTT = 65 * KB
OFF_XNT = 137 * KB
OFF_XRES = 137 * KB
OFF_XNEW = 153 * KB
OFF_W1B = 169 * KB
OFF_W3B = 177 * KB
OFF_TMPS = 185 * KB
OFF_PBUF = 189 * KB
OFF_STA = 169 * KB
OFF_STB = 173 * KB
OFF_STC = 177 * KB
OFF_ST16 = 181 * KB
OFF_QRAW = 177 * KB
OFF_QP = 179 * KB
OFF_T1 = 181 * KB
OFF_T2 = 185 * KB
OFF_CONST = 198 * KB


def _hadamard(n):
    i = np.arange(n)
    m = i[:, None] & i[None, :]
    bits = np.zeros_like(m)
    mm = m.copy()
    while mm.any():
        bits += mm & 1
        mm >>= 1
    return np.where(bits % 2 == 0, 1.0, -1.0).astype(np.float32)


def _rope_tables():
    inv_freq = 1.0 / THETA ** (np.arange(0, HD, 2, dtype=np.float32) / HD)
    pos = np.arange(S, dtype=np.float32)
    i = np.arange(128)
    f = inv_freq[(i % 64) % 32]
    ang = pos[None, :] * f[:, None]
    cosT = np.cos(ang).astype(np.float32)
    sgn = np.where((i % 64) < 32, -1.0, 1.0).astype(np.float32)
    sinS = (np.sin(ang) * sgn[:, None]).astype(np.float32)
    return cosT, sinS


def _rot_matrix():
    R = np.zeros((128, 128), np.float32)
    for m in range(128):
        R[(m // 64) * 64 + ((m % 64) ^ 32), m] = 1.0
    return R


def _wt_chunks(w, nct):
    a = w.reshape(nct, 128, NDT, 128)
    return np.ascontiguousarray(a.transpose(0, 3, 2, 1)).reshape(nct, 128, NDT * 128)


def host_prepare(inp):
    out = {}
    jp = np.arange(HF)
    hperm = (jp % 64) * 128 + (jp // 64)
    out["a1"] = _wt_chunks(np.asarray(inp["L0_ffn_W1"])[hperm, :], NHT)
    out["a3"] = _wt_chunks(np.asarray(inp["L0_ffn_W3"])[hperm, :], NHT)
    out["w2t"] = np.ascontiguousarray(np.asarray(inp["L0_ffn_W2"]).T)
    out["aq"] = _wt_chunks(np.asarray(inp["L0_attn_Wq"]), 16)
    out["ak"] = _wt_chunks(np.asarray(inp["L0_attn_Wk"]), 4)
    out["av"] = np.ascontiguousarray(np.asarray(inp["L0_attn_Wv"]).T.reshape(NDT, 128, 512))
    wo = np.asarray(inp["L0_attn_Wo"]).reshape(NDT, 128, NDT, 128)
    out["ao"] = np.ascontiguousarray(wo.transpose(0, 3, 2, 1)).reshape(NDT, 128, NDT * 128)
    out["h128"] = _hadamard(128).astype(BF)
    out["h64"] = _hadamard(64).astype(BF)
    cosT, sinS = _rope_tables()
    out["cosT"] = cosT
    out["sinS"] = sinS
    out["rotm"] = _rot_matrix().astype(BF)
    out["tril"] = np.triu(np.ones((128, 128), np.float32)).astype(BF)
    return out


def build_nc(debug_taps=(), stages="full"):
    nc = bass.Bass("TRN2", target_bir_lowering=False, debug=False)
    uid = [0]

    rbase = nc.bump_sbuf(201 * KB)[0]
    rbase = (rbase + 127) // 128 * 128

    def sb(name, shape, dtype, off):
        uid[0] += 1
        return nc.alloc_sbuf_tensor_at(f"{name}{uid[0]}", list(shape), dtype, offset=rbase + off)

    class SemH:

        def __init__(self, handle):
            self.h = handle
            self.base = 0
            self.pending = 0

        def add(self, n):
            self.pending += n
            return n

    sem_pool = {}

    def sems(*names):
        out = []
        for n in names:
            if n not in sem_pool:
                sem_pool[n] = SemH(nc.alloc_semaphore(f"sem_{n}"))
            out.append(sem_pool[n])
        return out

    def end_phase():
        nc.all_engine_barrier()
        for s in sem_pool.values():
            s.base += s.pending
            s.pending = 0

    class Chain:

        def __init__(self, sem):
            self.sem = sem

        def seal(self, ins):
            ins.then_inc(self.sem.h, self.sem.add(1))
            return ins

        def wait(self, eng):
            eng.wait_ge(self.sem.h, self.sem.base + self.sem.pending)

    xin = nc.dram_tensor("xin", [D, S], F32, kind="ExternalInput").ap()
    a1 = nc.dram_tensor("a1", [NHT, 128, NDT * 128], F32, kind="ExternalInput").ap()
    a3 = nc.dram_tensor("a3", [NHT, 128, NDT * 128], F32, kind="ExternalInput").ap()
    w2t = nc.dram_tensor("w2t", [HF, D], F32, kind="ExternalInput").ap()
    aq = nc.dram_tensor("aq", [16, 128, NDT * 128], F32, kind="ExternalInput").ap()
    ak = nc.dram_tensor("ak", [4, 128, NDT * 128], F32, kind="ExternalInput").ap()
    av = nc.dram_tensor("av", [NDT, 128, 512], F32, kind="ExternalInput").ap()
    ao = nc.dram_tensor("ao", [NDT, 128, NDT * 128], F32, kind="ExternalInput").ap()
    h128_in = nc.dram_tensor("h128", [128, 128], BF16, kind="ExternalInput").ap()
    h64_in = nc.dram_tensor("h64", [64, 64], BF16, kind="ExternalInput").ap()
    cosT_in = nc.dram_tensor("cosT", [128, S], F32, kind="ExternalInput").ap()
    sinS_in = nc.dram_tensor("sinS", [128, S], F32, kind="ExternalInput").ap()
    rotm_in = nc.dram_tensor("rotm", [128, 128], BF16, kind="ExternalInput").ap()
    tril_in = nc.dram_tensor("tril", [128, 128], BF16, kind="ExternalInput").ap()

    u_s = nc.dram_tensor("u_s", [HF, D], BF16, kind="Internal").ap()
    p4 = nc.dram_tensor("p4", [HF, D], BF16, kind="Internal").ap()
    xa = nc.dram_tensor("xa", [D, S], F32, kind="Internal").ap()
    xb = nc.dram_tensor("xb", [D, S], F32, kind="Internal").ap()
    qs = nc.dram_tensor("qs", [16, 128, S], BF16, kind="Internal").ap()
    den_s = nc.dram_tensor("den_s", [128, 512], F32, kind="Internal").ap()
    rc_s = nc.dram_tensor("rc_s", [128, 512], F32, kind="Internal").ap()
    yT = nc.dram_tensor("yT", [D, S], F32, kind="ExternalOutput").ap()

    pall = nc.alloc_psum_tensor("pall", [128, 4096], F32)

    def bank(b):
        return pall[:, 512 * b:512 * (b + 1)]

    ones_col = sb("ones_col", [128, 1], BF16, OFF_CONST)
    ones_row = sb("ones_row", [1, 128], BF16, OFF_CONST + 64)
    h128 = sb("h128c", [128, 128], BF16, OFF_CONST + 384)
    h64 = sb("h64c", [64, 64], BF16, OFF_CONST + 640)
    rotm = sb("rotmc", [128, 128], BF16, OFF_CONST + 768)
    tril = sb("trilc", [128, 128], BF16, OFF_CONST + 1024)

    XNT = sb("xnT", [128, NDT * SH], BF16, OFF_XNT)
    HID = sb("hid", [128, NHT * SH], BF16, OFF_HID)
    W1B = sb("w1b", [128, 2 * 2048], BF16, OFF_W1B)
    W3B = sb("w3b", [128, 2 * 2048], BF16, OFF_W3B)
    TMPS = sb("tmps", [128, 4 * 512], BF16, OFF_TMPS)
    PBUF = sb("pbuf", [128, 2 * 2048], BF16, OFF_PBUF)
    XRES = sb("xres", [128, 8 * 512], F32, OFF_XRES)
    XNEW = sb("xnew", [128, 8 * 512], F32, OFF_XNEW)
    COSS = sb("cosS", [128, S], F32, OFF_COSS)
    SINSS = sb("sinS", [128, S], F32, OFF_SINS)
    KTT = sb("kT", [128, 4 * S], BF16, OFF_KT)
    VAUG = sb("vaug", [128, 16 * 520], BF16, OFF_VAUG)
    WVB = sb("wvb", [128, HF], BF16, OFF_WVB)
    OUTT = sb("outT", [128, NDT * S], BF16, OFF_OUTT)

    (s_c,) = sems("c")
    with nc.Block() as blk:
        @blk.gpsimd
        def _(g):
            g.dma_start(h128[:], h128_in).then_inc(s_c.h, s_c.add(16))
            g.dma_start(h64[:], h64_in).then_inc(s_c.h, s_c.add(16))
            g.dma_start(rotm[:], rotm_in).then_inc(s_c.h, s_c.add(16))
            g.dma_start(tril[:], tril_in).then_inc(s_c.h, s_c.add(16))
            g.memset(ones_col[:], 1.0)
            g.memset(ones_row[:], 1.0)
            g.wait_ge(s_c.h, s_c.base + (64))
    end_phase()

    s1b = sb("s1b", [128, 2 * D], BF16, OFF_S1B)
    u_sb = sb("u_sb", [128, 2 * D], BF16, OFF_USB)
    s_ld0, s_ld1, s_mm, s_v, s_st0, s_st1 = sems("ld0", "ld1", "mm", "v", "st0", "st1")
    s_ld = [s_ld0, s_ld1]
    s_st = [s_st0, s_st1]
    with nc.Block() as blk:
        @blk.gpsimd
        def _(g):
            for bb in range(NHT):
                if bb >= 2:
                    g.wait_ge(s_mm.h, s_mm.base + (bb - 1))
                g.dma_start(s1b[:, 2048 * (bb % 2):2048 * (bb % 2 + 1)],
                            w2t[128 * bb:128 * (bb + 1), :]).then_inc(s_ld[bb % 2].h, s_ld[bb % 2].add(16))

        @blk.tensor
        def _(t):
            for bb in range(NHT):
                t.wait_ge(s_ld[bb % 2].h, s_ld[bb % 2].base + (16 * (bb // 2 + 1)))
                if bb >= 2:
                    t.wait_ge(s_v.h, s_v.base + (4 * (bb - 1)))
                for j in range(4):
                    ins = t.matmul(bank(4 * (bb % 2) + j), h128[:],
                                   s1b[:, 2048 * (bb % 2) + 512 * j:2048 * (bb % 2) + 512 * (j + 1)],
                                   start=True, stop=True)
                ins.then_inc(s_mm.h, s_mm.add(1))

        @blk.vector
        def _(v):
            for bb in range(NHT):
                v.wait_ge(s_mm.h, s_mm.base + (bb + 1))
                if bb >= 2:
                    v.wait_ge(s_st[bb % 2].h, s_st[bb % 2].base + (16 * ((bb - 2) // 2 + 1)))
                for j in range(4):
                    v.tensor_scalar_mul(
                        u_sb[:, 2048 * (bb % 2) + 512 * j:2048 * (bb % 2) + 512 * (j + 1)],
                        bank(4 * (bb % 2) + j), C_FWHT).then_inc(s_v.h, s_v.add(1))

        @blk.sync
        def _(s):
            for bb in range(NHT):
                s.wait_ge(s_v.h, s_v.base + (4 * (bb + 1)))
                s.dma_start(u_s[128 * bb:128 * (bb + 1), :],
                            u_sb[:, 2048 * (bb % 2):2048 * (bb % 2 + 1)]).then_inc(s_st[bb % 2].h, s_st[bb % 2].add(16))
            s.wait_ge(s_st0.h, s_st0.base + (16 * (NHT // 2)))
            s.wait_ge(s_st1.h, s_st1.base + (16 * (NHT // 2)))
    end_phase()

    if stages == "pb1":
        stages = "pbuild"
        skip_stage2 = True
    else:
        skip_stage2 = False
    v2b = sb("v2b", [64, 2 * D], BF16, OFF_V2B)
    p2sb = sb("p2sb", [64, 2 * D], BF16, OFF_P2SB)
    u_v = u_s.rearrange("(b p) d -> p b d", p=128)
    s_ld0, s_ld1, s_mm, s_v, s_st0, s_st1 = sems("ld2a", "ld2b", "mm", "v", "st2a", "st2b")
    s_ld = [s_ld0, s_ld1]
    s_st = [s_st0, s_st1]
    if skip_stage2:
        with nc.Block() as blk: pass
    with nc.Block() as blk:
        @blk.sync
        def _(s):
            for p in range(0 if skip_stage2 else 128):
                if p >= 2:
                    s.wait_ge(s_mm.h, s_mm.base + (p - 1))
                s.dma_start(v2b[:, 2048 * (p % 2):2048 * (p % 2 + 1)], u_v[p]).then_inc(s_ld[p % 2].h, s_ld[p % 2].add(16))

        @blk.tensor
        def _(t):
            for p in range(0 if skip_stage2 else 128):
                t.wait_ge(s_ld[p % 2].h, s_ld[p % 2].base + (16 * (p // 2 + 1)))
                if p >= 2:
                    t.wait_ge(s_v.h, s_v.base + (4 * (p - 1)))
                for j in range(4):
                    ins = t.matmul(bank(4 * (p % 2) + j)[0:64, :], h64[:],
                                   v2b[:, 2048 * (p % 2) + 512 * j:2048 * (p % 2) + 512 * (j + 1)],
                                   start=True, stop=True)
                ins.then_inc(s_mm.h, s_mm.add(1))

        @blk.vector
        def _(v):
            for p in range(0 if skip_stage2 else 128):
                v.wait_ge(s_mm.h, s_mm.base + (p + 1))
                if p >= 2:
                    v.wait_ge(s_st[p % 2].h, s_st[p % 2].base + (16 * ((p - 2) // 2 + 1)))
                for j in range(4):
                    v.tensor_copy(
                        p2sb[:, 2048 * (p % 2) + 512 * j:2048 * (p % 2) + 512 * (j + 1)],
                        bank(4 * (p % 2) + j)[0:64, :]).then_inc(s_v.h, s_v.add(1))

        @blk.scalar
        def _(sc):
            for p in range(0 if skip_stage2 else 128):
                sc.wait_ge(s_v.h, s_v.base + (4 * (p + 1)))
                sc.dma_start(p4[64 * p:64 * (p + 1), :],
                             p2sb[:, 2048 * (p % 2):2048 * (p % 2 + 1)]).then_inc(s_st[p % 2].h, s_st[p % 2].add(16))
            if not skip_stage2:
                sc.wait_ge(s_st0.h, s_st0.base + (16 * 64))
                sc.wait_ge(s_st1.h, s_st1.base + (16 * 64))
    end_phase()

    def emit_ln(src, sh):
        xTf = sb("xTf", [128, NDT * SH], F32, OFF_XTF)
        xb16 = sb("xb16", [128, NDT * SH], BF16, OFF_XB16)
        xsq = sb("xsq", [128, NDT * SH], BF16, OFF_XSQ)
        meanB = sb("meanB", [128, SH], F32, OFF_MEANB)
        rstdB = sb("rstdB", [128, SH], F32, OFF_RSTDB)
        stA = sb("stA", [1, SH], F32, OFF_STA)
        stB = sb("stB", [1, SH], F32, OFF_STB)
        stC = sb("stC", [1, SH], F32, OFF_STC)
        st16 = sb("st16", [1, 2 * SH], BF16, OFF_ST16)
        sxl = list(sems("sx0", "sx1", "sx2", "sx3"))
        sbl = list(sems("sb0", "sb1", "sb2", "sb3"))
        ssq, smm, svec, ssc2, svch = sems("ssq", "smm", "svec", "ssc2", "svch")
        col = SH

        with nc.Block() as blk:
            @blk.sync
            def _(s):
                for dt in range(NDT):
                    if dt >= 4:
                        s.wait_ge(ssq.h, ssq.base + (dt - 3))
                    s.dma_start(xTf[:, col * dt:col * (dt + 1)],
                                src[128 * dt:128 * (dt + 1), SH * sh:SH * (sh + 1)]).then_inc(sxl[dt % 4].h, sxl[dt % 4].add(16))

            @blk.gpsimd
            def _(g):
                for dt in range(NDT):
                    if dt >= 4:
                        g.wait_ge(smm.h, smm.base + (2 * (dt - 3)))
                    g.dma_start(xb16[:, col * dt:col * (dt + 1)],
                                src[128 * dt:128 * (dt + 1), SH * sh:SH * (sh + 1)]).then_inc(sbl[dt % 4].h, sbl[dt % 4].add(16))

            @blk.scalar
            def _(sc):
                for dt in range(NDT):
                    sc.wait_ge(sxl[dt % 4].h, sxl[dt % 4].base + (16 * (dt // 4 + 1)))
                    sc.activation(xsq[:, col * dt:col * (dt + 1)],
                                  xTf[:, col * dt:col * (dt + 1)], AF.Square).then_inc(ssq.h, ssq.add(1))
                sc.wait_ge(svec.h, svec.base + (1))
                sc.activation(stB[:], stB[:], AF.Sqrt).then_inc(ssc2.h, ssc2.add(1))

            @blk.tensor
            def _(t):
                for dt in range(NDT):
                    t.wait_ge(sbl[dt % 4].h, sbl[dt % 4].base + (16 * (dt // 4 + 1)))
                    for hf in range(2):
                        ins = t.matmul(bank(4 + hf)[0:1, :], ones_col[:],
                                       xb16[:, col * dt + 512 * hf:col * dt + 512 * (hf + 1)],
                                       start=(dt == 0), stop=(dt == NDT - 1), skip_group_check=True)
                    ins.then_inc(smm.h, smm.add(1))
                    t.wait_ge(ssq.h, ssq.base + (dt + 1))
                    for hf in range(2):
                        ins = t.matmul(bank(6 + hf)[0:1, :], ones_col[:],
                                       xsq[:, col * dt + 512 * hf:col * dt + 512 * (hf + 1)],
                                       start=(dt == 0), stop=(dt == NDT - 1), skip_group_check=True)
                    ins.then_inc(smm.h, smm.add(1))
                t.wait_ge(svec.h, svec.base + (2))
                for hf in range(2):
                    t.matmul(bank(4 + hf), ones_row[:],
                             st16[0:1, 512 * hf:512 * (hf + 1)], start=True, stop=True)
                for hf in range(2):
                    ins = t.matmul(bank(6 + hf), ones_row[:],
                                   st16[0:1, SH + 512 * hf:SH + 512 * (hf + 1)],
                                   start=True, stop=True)
                ins.then_inc(smm.h, smm.add(1))

            @blk.vector
            def _(v):
                ch = Chain(svch)
                v.wait_ge(smm.h, smm.base + (2 * NDT))
                inv = 1.0 / float(D)
                for hf in range(2):
                    ch.seal(v.tensor_scalar_mul(stA[0:1, 512 * hf:512 * (hf + 1)], bank(4 + hf)[0:1, :], inv))
                for hf in range(2):
                    ch.seal(v.tensor_scalar_mul(stB[0:1, 512 * hf:512 * (hf + 1)], bank(6 + hf)[0:1, :], inv))
                ch.wait(v)
                ch.seal(v.tensor_tensor(stC[:], stA[:], stA[:], op=ALU.mult))
                ch.wait(v)
                ch.seal(v.tensor_tensor(stB[:], stB[:], stC[:], op=ALU.subtract))
                ch.wait(v)
                v.tensor_scalar_add(stB[:], stB[:], EPS).then_inc(svec.h, svec.add(1))
                v.wait_ge(ssc2.h, ssc2.base + (1))
                ch.seal(v.reciprocal(stC[:], stB[:]))
                v.tensor_copy(st16[0:1, 0:SH], stA[:])
                ch.wait(v)
                v.tensor_copy(st16[0:1, SH:2 * SH], stC[:]).then_inc(svec.h, svec.add(1))
                v.wait_ge(smm.h, smm.base + (2 * NDT + 1))
                ch.seal(v.tensor_copy(meanB[:, :], pall[:, 4 * 512:6 * 512]))
                ch.seal(v.tensor_copy(rstdB[:, :], pall[:, 6 * 512:8 * 512]))
                ch.wait(v)
                for dt in range(NDT):
                    ch.seal(v.tensor_tensor(xTf[:, col * dt:col * (dt + 1)],
                                            xTf[:, col * dt:col * (dt + 1)], meanB[:], op=ALU.subtract))
                    ch.wait(v)
                    v.tensor_tensor(XNT[:, col * dt:col * (dt + 1)],
                                    xTf[:, col * dt:col * (dt + 1)], rstdB[:], op=ALU.mult)
        end_phase()

    def emit_up(sh):
        sw1 = list(sems("sw1a", "sw1b"))
        sw3 = list(sems("sw3a", "sw3b"))
        smg, sml, ssl, stt = sems("smg", "sml", "ssl", "stt")
        with nc.Block() as blk:
            @blk.gpsimd
            def _(g):
                for ht in range(NHT):
                    sl = 2048 * (ht % 2)
                    if ht >= 2:
                        g.wait_ge(smg.h, smg.base + (ht - 1))
                    g.dma_start(W1B[:, sl:sl + 2048], a1[ht]).then_inc(sw1[ht % 2].h, sw1[ht % 2].add(16))
                    if ht >= 2:
                        g.wait_ge(sml.h, sml.base + (ht - 1))
                    g.dma_start(W3B[:, sl:sl + 2048], a3[ht]).then_inc(sw3[ht % 2].h, sw3[ht % 2].add(16))

            @blk.tensor
            def _(t):
                for ht in range(NHT):
                    sl = 2048 * (ht % 2)
                    t.wait_ge(sw1[ht % 2].h, sw1[ht % 2].base + (16 * (ht // 2 + 1)))
                    if ht >= 1:
                        t.wait_ge(ssl.h, ssl.base + (2 * ht))
                    for dt in range(NDT):
                        for hf in range(2):
                            ins = t.matmul(bank(hf),
                                           W1B[:, sl + 128 * dt:sl + 128 * (dt + 1)],
                                           XNT[:, SH * dt + 512 * hf:SH * dt + 512 * (hf + 1)],
                                           start=(dt == 0), stop=(dt == NDT - 1),
                                           skip_group_check=True)
                    ins.then_inc(smg.h, smg.add(1))
                    t.wait_ge(sw3[ht % 2].h, sw3[ht % 2].base + (16 * (ht // 2 + 1)))
                    if ht >= 1:
                        t.wait_ge(stt.h, stt.base + (2 * ht))
                    for dt in range(NDT):
                        for hf in range(2):
                            ins = t.matmul(bank(2 + hf),
                                           W3B[:, sl + 128 * dt:sl + 128 * (dt + 1)],
                                           XNT[:, SH * dt + 512 * hf:SH * dt + 512 * (hf + 1)],
                                           start=(dt == 0), stop=(dt == NDT - 1),
                                           skip_group_check=True)
                    ins.then_inc(sml.h, sml.add(1))

            @blk.scalar
            def _(sc):
                for ht in range(NHT):
                    sc.wait_ge(smg.h, smg.base + (ht + 1))
                    if ht >= 2:
                        sc.wait_ge(stt.h, stt.base + (2 * (ht - 1)))
                    for hf in range(2):
                        sc.activation(
                            TMPS[:, 1024 * (ht % 2) + 512 * hf:1024 * (ht % 2) + 512 * (hf + 1)],
                            bank(hf), AF.Silu).then_inc(ssl.h, ssl.add(1))

            @blk.vector
            def _(v):
                for ht in range(NHT):
                    v.wait_ge(ssl.h, ssl.base + (2 * (ht + 1)))
                    v.wait_ge(sml.h, sml.base + (ht + 1))
                    for hf in range(2):
                        v.tensor_tensor(
                            HID[:, SH * ht + 512 * hf:SH * ht + 512 * (hf + 1)],
                            TMPS[:, 1024 * (ht % 2) + 512 * hf:1024 * (ht % 2) + 512 * (hf + 1)],
                            bank(2 + hf), op=ALU.mult).then_inc(stt.h, stt.add(1))
        end_phase()

    def emit_down(src, dst, sh):
        spl = list(sems("spa", "spb"))
        sxrl = list(sems("sxra", "sxrb"))
        sstl = list(sems("ssta", "sstb"))
        smd, svd = sems("smd", "svd")
        with nc.Block() as blk:
            @blk.sync
            def _(s):
                for dg in range(8):
                    for hb in range(8):
                        it = 8 * dg + hb
                        if it >= 2:
                            s.wait_ge(smd.h, smd.base + (it - 1))
                        srcv = p4[1024 * hb:1024 * (hb + 1), 256 * dg:256 * (dg + 1)] \
                            .rearrange("(h p) d -> p h d", p=128)
                        s.dma_start(
                            PBUF[:, 2048 * (it % 2):2048 * (it % 2 + 1)].rearrange(
                                "p (h d) -> p h d", h=8), srcv).then_inc(spl[it % 2].h, spl[it % 2].add(16))

            @blk.scalar
            def _(sc):
                for dg in range(8):
                    if dg >= 2:
                        sc.wait_ge(svd.h, svd.base + (4 * (dg - 1)))
                    for tt in range(4):
                        dc, hf = tt // 2, tt % 2
                        dch = 2 * dg + dc
                        sc.dma_start(
                            XRES[:, 512 * (4 * (dg % 2) + tt):512 * (4 * (dg % 2) + tt + 1)],
                            src[128 * dch:128 * (dch + 1),
                                SH * sh + 512 * hf:SH * sh + 512 * (hf + 1)]).then_inc(sxrl[dg % 2].h, sxrl[dg % 2].add(16))
                    if dg >= 1:
                        for tt in range(4):
                            dc, hf = tt // 2, tt % 2
                            dch = 2 * (dg - 1) + dc
                            sc.wait_ge(svd.h, svd.base + (4 * (dg - 1) + tt + 1))
                            sc.dma_start(
                                dst[128 * dch:128 * (dch + 1),
                                    SH * sh + 512 * hf:SH * sh + 512 * (hf + 1)],
                                XNEW[:, 512 * (4 * ((dg - 1) % 2) + tt):512 * (4 * ((dg - 1) % 2) + tt + 1)]
                            ).then_inc(sstl[(dg - 1) % 2].h, sstl[(dg - 1) % 2].add(16))
                for tt in range(4):
                    dc, hf = tt // 2, tt % 2
                    dch = 14 + dc
                    sc.wait_ge(svd.h, svd.base + (28 + tt + 1))
                    sc.dma_start(
                        dst[128 * dch:128 * (dch + 1),
                            SH * sh + 512 * hf:SH * sh + 512 * (hf + 1)],
                        XNEW[:, 512 * (4 + tt):512 * (4 + tt + 1)]).then_inc(sstl[1].h, sstl[1].add(16))
                sc.wait_ge(sstl[0].h, sstl[0].base + (16 * 16))
                sc.wait_ge(sstl[1].h, sstl[1].base + (16 * 16))

            @blk.tensor
            def _(t):
                for dg in range(8):
                    if dg >= 2:
                        t.wait_ge(svd.h, svd.base + (4 * (dg - 1)))
                    for hb in range(8):
                        it = 8 * dg + hb
                        t.wait_ge(spl[it % 2].h, spl[it % 2].base + (16 * (it // 2 + 1)))
                        for hl in range(8):
                            ht = 8 * hb + hl
                            for tt in range(4):
                                dc, hf = tt // 2, tt % 2
                                ins = t.matmul(
                                    bank(4 * (dg % 2) + tt),
                                    PBUF[:, 2048 * (it % 2) + 256 * hl + 128 * dc:
                                         2048 * (it % 2) + 256 * hl + 128 * (dc + 1)],
                                    HID[:, SH * ht + 512 * hf:SH * ht + 512 * (hf + 1)],
                                    start=(ht == 0), stop=(ht == NHT - 1),
                                    skip_group_check=True)
                        ins.then_inc(smd.h, smd.add(1))

            @blk.vector
            def _(v):
                for dg in range(8):
                    v.wait_ge(smd.h, smd.base + (8 * dg + 8))
                    v.wait_ge(sxrl[dg % 2].h, sxrl[dg % 2].base + (64 * (dg // 2 + 1)))
                    if dg >= 2:
                        v.wait_ge(sstl[dg % 2].h, sstl[dg % 2].base + (64 * ((dg - 2) // 2 + 1)))
                    for tt in range(4):
                        v.tensor_tensor(
                            XNEW[:, 512 * (4 * (dg % 2) + tt):512 * (4 * (dg % 2) + tt + 1)],
                            XRES[:, 512 * (4 * (dg % 2) + tt):512 * (4 * (dg % 2) + tt + 1)],
                            bank(4 * (dg % 2) + tt), op=ALU.add).then_inc(svd.h, svd.add(1))
        end_phase()

    def emit_ln_stream(src, sh):
        stA = sb("stA", [1, SH], F32, OFF_W1B)
        stB = sb("stB", [1, SH], F32, OFF_W1B + 4 * KB)
        stC = sb("stC", [1, SH], F32, OFF_W1B + 8 * KB)
        st16 = sb("st16", [1, 2 * SH], BF16, OFF_W1B + 12 * KB)
        bb16 = sb("bb16", [128, 2 * SH], BF16, OFF_W1B + 16 * KB)
        bsq = sb("bsq", [128, 2 * SH], BF16, OFF_W1B + 20 * KB)
        meanB = sb("meanB", [128, SH], F32, OFF_W1B + 16 * KB)
        rstdB = sb("rstdB", [128, SH], F32, OFF_W1B + 20 * KB)
        xTfb = sb("xTfb", [128, 2 * SH], F32, OFF_W1B)

        sbl2 = list(sems("sbsa", "sbsb"))
        sxfl = list(sems("sxfa", "sxfb"))
        ssq, smm, svec, ssc2, sap, svch2 = sems("ssq2", "smm2", "svec2", "ssc22", "sap", "svch2")
        with nc.Block() as blk:
            @blk.gpsimd
            def _(g):
                for dt in range(NDT):
                    if dt >= 2:
                        g.wait_ge(smm.h, smm.base + 2 * (dt - 1))
                    g.dma_start(bb16[:, SH * (dt % 2):SH * (dt % 2 + 1)],
                                src[128 * dt:128 * (dt + 1), SH * sh:SH * (sh + 1)]).then_inc(sbl2[dt % 2].h, sbl2[dt % 2].add(16))

            @blk.scalar
            def _(sc):
                for dt in range(NDT):
                    sc.wait_ge(sbl2[dt % 2].h, sbl2[dt % 2].base + (16 * (dt // 2 + 1)))
                    if dt >= 2:
                        sc.wait_ge(smm.h, smm.base + (2 * (dt - 1)))
                    sc.activation(bsq[:, SH * (dt % 2):SH * (dt % 2 + 1)],
                                  bb16[:, SH * (dt % 2):SH * (dt % 2 + 1)], AF.Square).then_inc(ssq.h, ssq.add(1))
                sc.wait_ge(svec.h, svec.base + (1))
                sc.activation(stB[:], stB[:], AF.Sqrt).then_inc(ssc2.h, ssc2.add(1))

            @blk.tensor
            def _(t):
                for dt in range(NDT):
                    t.wait_ge(ssq.h, ssq.base + (dt + 1))
                    for hf in range(2):
                        ins = t.matmul(bank(4 + hf)[0:1, :], ones_col[:],
                                       bb16[:, SH * (dt % 2) + 512 * hf:SH * (dt % 2) + 512 * (hf + 1)],
                                       start=(dt == 0), stop=(dt == NDT - 1), skip_group_check=True)
                    ins.then_inc(smm.h, smm.add(1))
                    for hf in range(2):
                        ins = t.matmul(bank(6 + hf)[0:1, :], ones_col[:],
                                       bsq[:, SH * (dt % 2) + 512 * hf:SH * (dt % 2) + 512 * (hf + 1)],
                                       start=(dt == 0), stop=(dt == NDT - 1), skip_group_check=True)
                    ins.then_inc(smm.h, smm.add(1))
                t.wait_ge(svec.h, svec.base + (2))
                for hf in range(2):
                    t.matmul(bank(4 + hf), ones_row[:],
                             st16[0:1, 512 * hf:512 * (hf + 1)], start=True, stop=True)
                for hf in range(2):
                    ins = t.matmul(bank(6 + hf), ones_row[:],
                                   st16[0:1, SH + 512 * hf:SH + 512 * (hf + 1)],
                                   start=True, stop=True)
                ins.then_inc(smm.h, smm.add(1))

            @blk.vector
            def _(v):
                ch = Chain(svch2)
                v.wait_ge(smm.h, smm.base + (2 * NDT))
                inv = 1.0 / float(D)
                for hf in range(2):
                    ch.seal(v.tensor_scalar_mul(stA[0:1, 512 * hf:512 * (hf + 1)], bank(4 + hf)[0:1, :], inv))
                for hf in range(2):
                    ch.seal(v.tensor_scalar_mul(stB[0:1, 512 * hf:512 * (hf + 1)], bank(6 + hf)[0:1, :], inv))
                ch.wait(v)
                ch.seal(v.tensor_tensor(stC[:], stA[:], stA[:], op=ALU.mult))
                ch.wait(v)
                ch.seal(v.tensor_tensor(stB[:], stB[:], stC[:], op=ALU.subtract))
                ch.wait(v)
                v.tensor_scalar_add(stB[:], stB[:], EPS).then_inc(svec.h, svec.add(1))
                v.wait_ge(ssc2.h, ssc2.base + (1))
                ch.seal(v.reciprocal(stC[:], stB[:]))
                v.tensor_copy(st16[0:1, 0:SH], stA[:])
                ch.wait(v)
                v.tensor_copy(st16[0:1, SH:2 * SH], stC[:]).then_inc(svec.h, svec.add(1))
                v.wait_ge(smm.h, smm.base + (2 * NDT + 1))
                ch.seal(v.tensor_copy(meanB[:, :], pall[:, 4 * 512:6 * 512]))
                ch.seal(v.tensor_copy(rstdB[:, :], pall[:, 6 * 512:8 * 512]))
                ch.wait(v)
                v.engine_nop().then_inc(svec.h, svec.add(1))
                for dt in range(NDT):
                    v.wait_ge(sxfl[dt % 2].h, sxfl[dt % 2].base + (16 * (dt // 2 + 1)))
                    ch.seal(v.tensor_tensor(xTfb[:, SH * (dt % 2):SH * (dt % 2 + 1)],
                                            xTfb[:, SH * (dt % 2):SH * (dt % 2 + 1)], meanB[:], op=ALU.subtract))
                    ch.wait(v)
                    v.tensor_tensor(XNT[:, SH * dt:SH * (dt + 1)],
                                    xTfb[:, SH * (dt % 2):SH * (dt % 2 + 1)], rstdB[:],
                                    op=ALU.mult).then_inc(sap.h, sap.add(1))

            @blk.sync
            def _(s):
                s.wait_ge(svec.h, svec.base + 3)
                for dt in range(NDT):
                    if dt >= 2:
                        s.wait_ge(sap.h, sap.base + (dt - 1))
                    s.dma_start(xTfb[:, SH * (dt % 2):SH * (dt % 2 + 1)],
                                src[128 * dt:128 * (dt + 1), SH * sh:SH * (sh + 1)]).then_inc(sxfl[dt % 2].h, sxfl[dt % 2].add(16))
        end_phase()

    def emit_rope_consts():
        (s_c2,) = sems("c2")
        with nc.Block() as blk:
            @blk.sync
            def _(s):
                s.dma_start(COSS[:], cosT_in).then_inc(s_c2.h, s_c2.add(16))
                s.dma_start(SINSS[:], sinS_in).then_inc(s_c2.h, s_c2.add(16))
                s.wait_ge(s_c2.h, s_c2.base + (32))
        end_phase()

    def emit_qkv(sh):
        QRAW = sb("qraw", [128, 2 * 512], BF16, OFF_QRAW)
        QP = sb("qp", [128, 2 * 512], BF16, OFF_QP)
        T1 = sb("t1", [128, 2 * 512], F32, OFF_T1)
        T2 = sb("t2", [128, 2 * 512], F32, OFF_T2)

        tiles = [("q", ct, hf) for ct in range(16) for hf in range(2)] + \
                [("k", ct, hf) for ct in range(4) for hf in range(2)]

        swql = list(sems("swqa", "swqb"))
        sqsl = list(sems("sqsa", "sqsb"))
        smq, smr, sv1, sv2, sv3, smv, svv, swv, svch3 = sems(
            "smq", "smr", "sv1", "sv2", "sv3", "smv", "svv", "swv", "svch3")

        with nc.Block() as blk:
            @blk.gpsimd
            def _(g):
                for ci in range(20):
                    if ci >= 2:
                        g.wait_ge(smq.h, smq.base + (2 * (ci - 1)))
                    srcw = aq[ci] if ci < 16 else ak[ci - 16]
                    g.dma_start(W1B[:, 2048 * (ci % 2):2048 * (ci % 2 + 1)], srcw).then_inc(swql[ci % 2].h, swql[ci % 2].add(16))
                for dt in range(NDT):
                    g.dma_start(WVB[:, 512 * dt:512 * (dt + 1)], av[dt]).then_inc(swv.h, swv.add(16))

            @blk.tensor
            def _(t):
                def proj_mm(i):
                    kind, ct, hf = tiles[i]
                    ci = i // 2
                    par = i % 2
                    t.wait_ge(swql[ci % 2].h, swql[ci % 2].base + (16 * (ci // 2 + 1)))
                    if i >= 2:
                        t.wait_ge(sv1.h, sv1.base + (i - 1))
                    for dt in range(NDT):
                        ins = t.matmul(bank(par),
                                       W1B[:, 2048 * (ci % 2) + 128 * dt:2048 * (ci % 2) + 128 * (dt + 1)],
                                       XNT[:, SH * dt + 512 * hf:SH * dt + 512 * (hf + 1)],
                                       start=(dt == 0), stop=(dt == NDT - 1),
                                       skip_group_check=True)
                    ins.then_inc(smq.h, smq.add(1))

                proj_mm(0)
                for i, (kind, ct, hf) in enumerate(tiles):
                    par = i % 2
                    if i + 1 < len(tiles):
                        proj_mm(i + 1)
                    t.wait_ge(sv1.h, sv1.base + (i + 1))
                    if i >= 2:
                        t.wait_ge(sv2.h, sv2.base + (i - 1))
                    t.matmul(bank(2 + par), rotm[:],
                             QRAW[:, 512 * par:512 * (par + 1)],
                             start=True, stop=True).then_inc(smr.h, smr.add(1))
                t.wait_ge(swv.h, swv.base + (16 * NDT))
                for st in range(8):
                    if st >= 2:
                        t.wait_ge(svv.h, svv.base + (st - 1))
                    for dt in range(NDT):
                        ins = t.matmul(bank(4 + st % 2),
                                       XNT[:, SH * dt + 128 * st:SH * dt + 128 * (st + 1)],
                                       WVB[:, 512 * dt:512 * (dt + 1)],
                                       start=(dt == 0), stop=(dt == NDT - 1),
                                       skip_group_check=True)
                    ins.then_inc(smv.h, smv.add(1))

            @blk.vector
            def _(v):
                ch = Chain(svch3)
                for i, (kind, ct, hf) in enumerate(tiles):
                    par = i % 2
                    so = SH * sh + 512 * hf
                    v.wait_ge(smq.h, smq.base + (i + 1))
                    if i >= 2:
                        v.wait_ge(smr.h, smr.base + (i - 1))
                    ch.seal(v.tensor_copy(QRAW[:, 512 * par:512 * (par + 1)], bank(par)))
                    ch.wait(v)
                    v.engine_nop().then_inc(sv1.h, sv1.add(1))
                    ch.seal(v.tensor_tensor(T1[:, 512 * par:512 * (par + 1)],
                                            QRAW[:, 512 * par:512 * (par + 1)],
                                            COSS[:, so:so + 512], op=ALU.mult))
                    v.wait_ge(smr.h, smr.base + (i + 1))
                    ch.seal(v.tensor_tensor(T2[:, 512 * par:512 * (par + 1)], bank(2 + par),
                                            SINSS[:, so:so + 512], op=ALU.mult))
                    ch.wait(v)
                    v.engine_nop().then_inc(sv2.h, sv2.add(1))
                    if kind == "q":
                        if i >= 2:
                            v.wait_ge(sqsl[i % 2].h, sqsl[i % 2].base + (16 * ((i - 2) // 2 + 1)))
                        v.tensor_tensor(QP[:, 512 * par:512 * (par + 1)],
                                        T1[:, 512 * par:512 * (par + 1)],
                                        T2[:, 512 * par:512 * (par + 1)], op=ALU.add).then_inc(sv3.h, sv3.add(1))
                    else:
                        v.tensor_tensor(KTT[:, S * ct + so:S * ct + so + 512],
                                        T1[:, 512 * par:512 * (par + 1)],
                                        T2[:, 512 * par:512 * (par + 1)], op=ALU.add).then_inc(sv3.h, sv3.add(1))
                for st in range(8):
                    stg = 8 * sh + st
                    v.wait_ge(smv.h, smv.base + (st + 1))
                    ch.seal(v.memset(VAUG[:, 520 * stg:520 * (stg + 1)], 1.0))
                    ch.wait(v)
                    v.tensor_copy(
                        VAUG[:, 520 * stg:520 * (stg + 1)].rearrange(
                            "p (g k) -> p g k", g=8, k=65)[:, :, 0:64],
                        bank(4 + st % 2).rearrange("p (g k) -> p g k", g=8),
                    ).then_inc(svv.h, svv.add(1))

            @blk.sync
            def _(s):
                for i, (kind, ct, hf) in enumerate(tiles):
                    if kind != "q":
                        continue
                    par = i % 2
                    s.wait_ge(sv3.h, sv3.base + (i + 1))
                    s.dma_start(qs[ct][:, SH * sh + 512 * hf:SH * sh + 512 * (hf + 1)],
                                QP[:, 512 * par:512 * (par + 1)]).then_inc(sqsl[i % 2].h, sqsl[i % 2].add(16))
                s.wait_ge(sqsl[0].h, sqsl[0].base + (16 * 16))
                s.wait_ge(sqsl[1].h, sqsl[1].base + (16 * 16))
        end_phase()

    def emit_scores():
        QBUF = sb("qbuf", [128, 2 * 512], BF16, OFF_QBUF)
        PRB = sb("prb", [128, 4 * 512], BF16, OFF_PRB)
        DENS = sb("dens", [128, 2 * 1024], F32, OFF_T1)

        sqll = list(sems("sqla", "sqlb"))
        sdsl = list(sems("sdsa", "sdsb"))
        sms, sse, smk, smav, sout, sdnv = sems(
            "sms", "sse", "smk", "smav", "sout", "sdnv")

        its = [(c, dkt, j) for c in range(4) for dkt in range(4) for j in range(4)]
        NIT = len(its)

        sms_h, sse_h, smk_h = {}, {}, {}
        av_cum = [0] * (NIT + 1)
        c_sms = c_sse = c_smk = 0
        for idx, (c, dkt, j) in enumerate(its):
            nkt = 4 * c + 4
            for kt in range(nkt):
                c_sms += 1
                sms_h[(idx, kt)] = c_sms
                c_sse += 1
                sse_h[(idx, kt)] = c_sse
                if kt >= 4 * c:
                    c_smk += 1
                smk_h[(idx, kt)] = c_smk
            av_cum[idx + 1] = av_cum[idx] + 2 * nkt
        sms_cum = {idx: sms_h[(idx, 4 * its[idx][0] + 3)] for idx in range(NIT)}

        def prb_ap(par, off):
            return PRB[:, 512 * (2 * par):512 * (2 * par) + 1024].rearrange(
                "p (b k) -> p b k", b=2)[:, :, off:512]

        def sco_ap(par, off):
            return pall[:, 512 * (2 * par):512 * (2 * par) + 1024].rearrange(
                "p (b k) -> p b k", b=2)[:, :, off:512]

        with nc.Block() as blk:
            @blk.sync
            def _(s):
                for idx, (c, dkt, j) in enumerate(its):
                    hA, hB = 8 * dkt + j, 8 * dkt + 4 + j
                    if idx >= 2:
                        s.wait_ge(sms.h, sms.base + (sms_cum[idx - 2]))
                    qb = 512 * (idx % 2)
                    for which, h in ((0, hA), (1, hB)):
                        s.dma_start(QBUF[64 * which:64 * (which + 1), qb:qb + 512],
                                    qs[h // 2][64 * (h % 2):64 * (h % 2) + 64, 512 * c:512 * (c + 1)]
                                    ).then_inc(sqll[idx % 2].h, sqll[idx % 2].add(16))
                    if idx >= 1:
                        s.wait_ge(sdnv.h, sdnv.base + (idx))
                        jp = (idx - 1) % 2
                        for w in range(2):
                            s.dma_start(den_s[2 * (idx - 1) + w:2 * (idx - 1) + w + 1, :],
                                        DENS[64:65, 1024 * jp + 512 * w:1024 * jp + 512 * (w + 1)]
                                        ).then_inc(sdsl[jp].h, sdsl[jp].add(16))
                s.wait_ge(sdnv.h, sdnv.base + (NIT))
                jp = (NIT - 1) % 2
                for w in range(2):
                    s.dma_start(den_s[2 * (NIT - 1) + w:2 * (NIT - 1) + w + 1, :],
                                DENS[64:65, 1024 * jp + 512 * w:1024 * jp + 512 * (w + 1)]
                                ).then_inc(sdsl[jp].h, sdsl[jp].add(16))
                s.wait_ge(sdsl[0].h, sdsl[0].base + (16 * 64))
                s.wait_ge(sdsl[1].h, sdsl[1].base + (16 * 64))

            @blk.tensor
            def _(t):
                for idx, (c, dkt, j) in enumerate(its):
                    nkt = 4 * c + 4
                    gA, gB = (8 * dkt + j) // 4, (8 * dkt + 4 + j) // 4
                    ip = idx % 2
                    qb = 512 * ip
                    t.wait_ge(sqll[ip].h, sqll[ip].base + (32 * (idx // 2 + 1)))
                    if idx >= 2:
                        t.wait_ge(sout.h, sout.base + (2 * (idx - 1)))
                    def score_mm(kt):
                        off = 128 * (kt - 4 * c) if kt >= 4 * c else 0
                        par = kt % 2
                        if kt >= 2:
                            t.wait_ge(sse.h, sse.base + (sse_h[(idx, kt - 2)]))
                        elif idx >= 1:
                            t.wait_ge(sse.h, sse.base + (sse_h[(idx - 1, 4 * its[idx - 1][0] + 3)]))
                        t.matmul(bank(2 * par)[:, off:512],
                                 KTT[0:64, S * dkt + 128 * kt:S * dkt + 128 * (kt + 1)],
                                 QBUF[0:64, qb + off:qb + 512],
                                 start=True, stop=True, tile_position=(0, 0))
                        ins = t.matmul(bank(2 * par + 1)[:, off:512],
                                       KTT[64:128, S * dkt + 128 * kt:S * dkt + 128 * (kt + 1)],
                                       QBUF[64:128, qb + off:qb + 512],
                                       start=True, stop=True, tile_position=(64, 0))
                        ins.then_inc(sms.h, sms.add(1))

                    score_mm(0)
                    for kt in range(nkt):
                        off = 128 * (kt - 4 * c) if kt >= 4 * c else 0
                        par = kt % 2
                        if kt + 1 < nkt:
                            score_mm(kt + 1)
                        t.wait_ge(sse.h, sse.base + (sse_h[(idx, kt)]))
                        if kt >= 4 * c:
                            t.wait_ge(smk.h, smk.base + (smk_h[(idx, kt)]))
                        t.matmul(bank(4 + 2 * ip)[0:65, off:512],
                                 VAUG[:, 520 * kt + 65 * gA:520 * kt + 65 * gA + 65],
                                 PRB[:, 512 * (2 * par) + off:512 * (2 * par) + 512],
                                 start=(kt == 0), stop=(kt == nkt - 1),
                                 skip_group_check=True).then_inc(smav.h, smav.add(1))
                        t.matmul(bank(5 + 2 * ip)[0:65, off:512],
                                 VAUG[:, 520 * kt + 65 * gB:520 * kt + 65 * gB + 65],
                                 PRB[:, 512 * (2 * par + 1) + off:512 * (2 * par + 1) + 512],
                                 start=(kt == 0), stop=(kt == nkt - 1),
                                 skip_group_check=True).then_inc(smav.h, smav.add(1))

            @blk.scalar
            def _(sc):
                for idx, (c, dkt, j) in enumerate(its):
                    nkt = 4 * c + 4
                    for kt in range(nkt):
                        off = 128 * (kt - 4 * c) if kt >= 4 * c else 0
                        par = kt % 2
                        sc.wait_ge(sms.h, sms.base + (sms_h[(idx, kt)]))
                        if kt >= 2:
                            sc.wait_ge(smav.h, smav.base + (av_cum[idx] + 2 * (kt - 1)))
                        elif idx >= 1:
                            sc.wait_ge(smav.h, smav.base + (av_cum[idx]))
                        sc.activation(prb_ap(par, off), sco_ap(par, off),
                                      AF.Exp, scale=0.125).then_inc(sse.h, sse.add(1))

            @blk.vector
            def _(v):
                trib = tril[:].rearrange("p (o k) -> p o k", o=1).broadcast_to((128, 2, 128))
                for idx, (c, dkt, j) in enumerate(its):
                    hA, hB = 8 * dkt + j, 8 * dkt + 4 + j
                    nkt = 4 * c + 4
                    ip = idx % 2
                    for kt in range(4 * c, nkt):
                        off = 128 * (kt - 4 * c)
                        par = kt % 2
                        v.wait_ge(sse.h, sse.base + (sse_h[(idx, kt)]))
                        v.tensor_tensor(prb_ap(par, off)[:, :, 0:128],
                                        prb_ap(par, off)[:, :, 0:128],
                                        trib, op=ALU.mult).then_inc(smk.h, smk.add(1))
                    v.wait_ge(smav.h, smav.base + (av_cum[idx + 1]))
                    if idx >= 2:
                        v.wait_ge(sdsl[ip].h, sdsl[ip].base + (32 * ((idx - 2) // 2 + 1)))
                    for which, h in ((0, hA), (1, hB)):
                        v.tensor_copy(
                            OUTT[64 * (h % 2):64 * (h % 2) + 64,
                                 S * (h // 2) + 512 * c:S * (h // 2) + 512 * (c + 1)],
                            bank(4 + which + 2 * ip)[0:64, :]).then_inc(sout.h, sout.add(1))
                    v.tensor_copy(
                        DENS[64:65, 1024 * ip:1024 * ip + 1024].rearrange("p (b k) -> p b k", b=2),
                        pall[:, 512 * (4 + 2 * ip):512 * (6 + 2 * ip)].rearrange(
                            "p (b k) -> p b k", b=2)[64:65, :, :],
                    ).then_inc(sdnv.h, sdnv.add(1))
        end_phase()

    def emit_norm():
        DENB = sb("denb", [128, 512], F32, OFF_T1)
        RCA = sb("rca", [128, 512], F32, OFF_T1 + 2 * KB)
        RCB = sb("rcb", [128, 4 * 512], F32, OFF_TMPS)

        its = [(c, dkt, j) for c in range(4) for dkt in range(4) for j in range(4)]
        sdl, svr, srw, svt = sems("sdl", "svr", "srw", "svt")
        srbl = list(sems("srb0", "srb1", "srb2", "srb3"))

        with nc.Block() as blk:
            @blk.sync
            def _(s):
                s.dma_start(DENB[:], den_s[:, :]).then_inc(sdl.h, sdl.add(16))
                s.wait_ge(svr.h, svr.base + (1))
                s.dma_start(rc_s[:, :], RCA[:]).then_inc(srw.h, srw.add(16))
                s.wait_ge(srw.h, srw.base + (16))
                for r in range(128):
                    if r >= 4:
                        s.wait_ge(svt.h, svt.base + (r - 3))
                    s.dma_start(RCB[:, 512 * (r % 4):512 * (r % 4 + 1)],
                                rc_s[r:r + 1, :].broadcast_to((128, 512))
                                ).then_inc(srbl[r % 4].h, srbl[r % 4].add(16))

            @blk.vector
            def _(v):
                v.wait_ge(sdl.h, sdl.base + (16))
                v.reciprocal(RCA[:], DENB[:]).then_inc(svr.h, svr.add(1))
                for r in range(128):
                    idx, which = r // 2, r % 2
                    c, dkt, j = its[idx]
                    h = 8 * dkt + j + 4 * which
                    v.wait_ge(srbl[r % 4].h, srbl[r % 4].base + (16 * (r // 4 + 1)))
                    hp = 64 * (h % 2)
                    sl = OUTT[hp:hp + 64,
                              S * (h // 2) + 512 * c:S * (h // 2) + 512 * (c + 1)]
                    v.tensor_tensor(sl, sl, RCB[hp:hp + 64, 512 * (r % 4):512 * (r % 4 + 1)],
                                    op=ALU.mult).then_inc(svt.h, svt.add(1))
        end_phase()

    def emit_outproj(src, dst):
        swol = list(sems("swoa", "swob"))
        sxr2l = list(sems("sxr2a", "sxr2b"))
        sst2l = list(sems("sst2a", "sst2b"))
        smo, svo = sems("smo", "svo")
        with nc.Block() as blk:
            @blk.gpsimd
            def _(g):
                for dch in range(NDT):
                    if dch >= 2:
                        g.wait_ge(smo.h, smo.base + (dch - 1))
                    g.dma_start(W3B[:, 2048 * (dch % 2):2048 * (dch % 2 + 1)],
                                ao[dch]).then_inc(swol[dch % 2].h, swol[dch % 2].add(16))

            @blk.scalar
            def _(sc):
                for dch in range(NDT):
                    if dch >= 2:
                        sc.wait_ge(svo.h, svo.base + (4 * (dch - 1)))
                    for hf in range(4):
                        sc.dma_start(
                            XRES[:, 512 * (4 * (dch % 2) + hf):512 * (4 * (dch % 2) + hf + 1)],
                            src[128 * dch:128 * (dch + 1), 512 * hf:512 * (hf + 1)]).then_inc(sxr2l[dch % 2].h, sxr2l[dch % 2].add(16))
                    if dch >= 1:
                        for hf in range(4):
                            sc.wait_ge(svo.h, svo.base + (4 * (dch - 1) + hf + 1))
                            sc.dma_start(
                                dst[128 * (dch - 1):128 * dch, 512 * hf:512 * (hf + 1)],
                                XNEW[:, 512 * (4 * ((dch - 1) % 2) + hf):512 * (4 * ((dch - 1) % 2) + hf + 1)]
                            ).then_inc(sst2l[(dch - 1) % 2].h, sst2l[(dch - 1) % 2].add(16))
                for hf in range(4):
                    sc.wait_ge(svo.h, svo.base + (60 + hf + 1))
                    sc.dma_start(dst[128 * 15:2048, 512 * hf:512 * (hf + 1)],
                                 XNEW[:, 512 * (4 + hf):512 * (4 + hf + 1)]).then_inc(sst2l[1].h, sst2l[1].add(16))
                sc.wait_ge(sst2l[0].h, sst2l[0].base + (16 * 32))
                sc.wait_ge(sst2l[1].h, sst2l[1].base + (16 * 32))

            @blk.tensor
            def _(t):
                for dch in range(NDT):
                    t.wait_ge(swol[dch % 2].h, swol[dch % 2].base + (16 * (dch // 2 + 1)))
                    if dch >= 2:
                        t.wait_ge(svo.h, svo.base + (4 * (dch - 1)))
                    for dvt in range(NDT):
                        for hf in range(4):
                            ins = t.matmul(
                                bank(4 * (dch % 2) + hf),
                                W3B[:, 2048 * (dch % 2) + 128 * dvt:2048 * (dch % 2) + 128 * (dvt + 1)],
                                OUTT[:, S * dvt + 512 * hf:S * dvt + 512 * (hf + 1)],
                                start=(dvt == 0), stop=(dvt == NDT - 1),
                                skip_group_check=True)
                    ins.then_inc(smo.h, smo.add(1))

            @blk.vector
            def _(v):
                for dch in range(NDT):
                    v.wait_ge(smo.h, smo.base + (dch + 1))
                    v.wait_ge(sxr2l[dch % 2].h, sxr2l[dch % 2].base + (64 * (dch // 2 + 1)))
                    if dch >= 2:
                        v.wait_ge(sst2l[dch % 2].h, sst2l[dch % 2].base + (64 * ((dch - 2) // 2 + 1)))
                    for hf in range(4):
                        v.tensor_tensor(
                            XNEW[:, 512 * (4 * (dch % 2) + hf):512 * (4 * (dch % 2) + hf + 1)],
                            XRES[:, 512 * (4 * (dch % 2) + hf):512 * (4 * (dch % 2) + hf + 1)],
                            bank(4 * (dch % 2) + hf), op=ALU.add).then_inc(svo.h, svo.add(1))
        end_phase()

    if stages != "pbuild":
        for sh in range(2):
            emit_ln(xin, sh)
            emit_up(sh)
            emit_down(xin, xa, sh)
        if stages != "ffn1":
            emit_rope_consts()
            for sh in range(2):
                emit_ln_stream(xa, sh)
                emit_qkv(sh)
            emit_scores()
            emit_norm()
            emit_outproj(xa, xb)
            if stages != "attn":
                for sh in range(2):
                    emit_ln(xb, sh)
                    emit_up(sh)
                    emit_down(xb, yT, sh)

    if debug_taps:
        (s_dt,) = sems("dt")
        taps = {"p4": (p4, [HF, D], BF16), "u_s": (u_s, [HF, D], BF16),
                "xa": (xa, [D, S], F32), "xb": (xb, [D, S], F32),
                "qs": (qs, [16, 128, S], BF16),
                "outT": (OUTT[:], [128, NDT * S], BF16),
                "kT": (KTT[:], [128, 4 * S], BF16),
                "vaug": (VAUG[:], [128, 16 * 520], BF16)}
        with nc.Block() as blk:
            @blk.sync
            def _(s):
                n = 0
                for name in debug_taps:
                    src_ap, shp, dt_ = taps[name]
                    dstp = nc.dram_tensor(f"tap_{name}", shp, dt_, kind="ExternalOutput").ap()
                    s.dma_start(dstp, src_ap).then_inc(s_dt.h, s_dt.add(16))
                    n += 1
                s.wait_ge(s_dt.h, s_dt.base + (16 * n))
        end_phase()

    nc.finalize()
    return nc


def make_in_maps(inputs):
    hs = np.asarray(inputs["L0_attn_hidden_states"])
    w = host_prepare(inputs)
    in_maps = []
    for b in range(B):
        m = {"xin": np.ascontiguousarray(hs[b].T)}
        for k in ("a1", "a3", "w2t", "aq", "ak", "av", "ao",
                  "h128", "h64", "cosT", "sinS", "rotm", "tril"):
            m[k] = w[k]
        in_maps.append(m)
    return in_maps


def kernel(**inputs):
    nc = build_nc()
    in_maps = make_in_maps(inputs)
    res = run_bass_kernel_spmd(nc, in_maps, core_ids=list(range(B)))
    out = np.empty((B, S, D), np.float32)
    for b in range(B):
        out[b] = res.results[b]["yT"].T
    return out


# version 06
# speedup vs baseline: 1.1398x; 1.0108x over previous
import numpy as np
import ml_dtypes
import concourse.bass as bass
import concourse.mybir as mybir
from concourse.bass_utils import run_bass_kernel_spmd

F32 = mybir.dt.float32
BF16 = mybir.dt.bfloat16
AF = mybir.ActivationFunctionType
ALU = mybir.AluOpType

B, S, D, HF = 8, 2048, 2048, 8192
NDT = D // 128
NHT = HF // 128
SH = S // 2
HD = 64
THETA = 10000.0
EPS = 1e-5
C_FWHT = 0.5 / float(np.sqrt(HF))
BF = ml_dtypes.bfloat16
KB = 1024

OFF_HID = 0
OFF_XTF = 0
OFF_XB16 = 64 * KB
OFF_XSQ = 96 * KB
OFF_MEANB = 128 * KB
OFF_RSTDB = 132 * KB
OFF_S1B = 0
OFF_USB = 8 * KB
OFF_V2B = 16 * KB
OFF_P2SB = 24 * KB
OFF_COSS = 0
OFF_SINS = 8 * KB
OFF_KT = 16 * KB
OFF_VAUG = 32 * KB
OFF_WVB = 49 * KB
OFF_QBUF = 49 * KB
OFF_PRB = 51 * KB
OFF_AVS = 55 * KB
OFF_RC = 59 * KB
OFF_RC16 = 63 * KB
OFF_OUTT = 65 * KB
OFF_XNT = 137 * KB
OFF_XRES = 137 * KB
OFF_XNEW = 153 * KB
OFF_W1B = 169 * KB
OFF_W3B = 177 * KB
OFF_TMPS = 185 * KB
OFF_PBUF = 189 * KB
OFF_STA = 169 * KB
OFF_STB = 173 * KB
OFF_STC = 177 * KB
OFF_ST16 = 181 * KB
OFF_QRAW = 177 * KB
OFF_QP = 179 * KB
OFF_T1 = 181 * KB
OFF_T2 = 185 * KB
OFF_CONST = 198 * KB


def _hadamard(n):
    i = np.arange(n)
    m = i[:, None] & i[None, :]
    bits = np.zeros_like(m)
    mm = m.copy()
    while mm.any():
        bits += mm & 1
        mm >>= 1
    return np.where(bits % 2 == 0, 1.0, -1.0).astype(np.float32)


def _rope_tables():
    inv_freq = 1.0 / THETA ** (np.arange(0, HD, 2, dtype=np.float32) / HD)
    pos = np.arange(S, dtype=np.float32)
    i = np.arange(128)
    f = inv_freq[(i % 64) % 32]
    ang = pos[None, :] * f[:, None]
    cosT = np.cos(ang).astype(np.float32)
    sgn = np.where((i % 64) < 32, -1.0, 1.0).astype(np.float32)
    sinS = (np.sin(ang) * sgn[:, None]).astype(np.float32)
    return cosT, sinS


def _rot_matrix():
    R = np.zeros((128, 128), np.float32)
    for m in range(128):
        R[(m // 64) * 64 + ((m % 64) ^ 32), m] = 1.0
    return R


def _wt_chunks(w, nct):
    a = w.reshape(nct, 128, NDT, 128)
    return np.ascontiguousarray(a.transpose(0, 3, 2, 1)).reshape(nct, 128, NDT * 128)


def host_prepare(inp):
    out = {}
    jp = np.arange(HF)
    hperm = (jp % 64) * 128 + (jp // 64)
    out["a1"] = _wt_chunks(np.asarray(inp["L0_ffn_W1"])[hperm, :], NHT)
    out["a3"] = _wt_chunks(np.asarray(inp["L0_ffn_W3"])[hperm, :], NHT)
    out["w2t"] = np.ascontiguousarray(np.asarray(inp["L0_ffn_W2"]).T)
    out["aq"] = _wt_chunks(np.asarray(inp["L0_attn_Wq"]), 16)
    out["ak"] = _wt_chunks(np.asarray(inp["L0_attn_Wk"]), 4)
    out["av"] = np.ascontiguousarray(np.asarray(inp["L0_attn_Wv"]).T.reshape(NDT, 128, 512))
    wo = np.asarray(inp["L0_attn_Wo"]).reshape(NDT, 128, NDT, 128)
    out["ao"] = np.ascontiguousarray(wo.transpose(0, 3, 2, 1)).reshape(NDT, 128, NDT * 128)
    out["h128"] = _hadamard(128).astype(BF)
    out["h64"] = _hadamard(64).astype(BF)
    cosT, sinS = _rope_tables()
    out["cosT"] = cosT
    out["sinS"] = sinS
    out["rotm"] = _rot_matrix().astype(BF)
    out["tril"] = np.triu(np.ones((128, 128), np.float32)).astype(BF)
    return out


def build_nc(debug_taps=(), stages="full"):
    nc = bass.Bass("TRN2", target_bir_lowering=False, debug=False)
    uid = [0]

    rbase = nc.bump_sbuf(201 * KB)[0]
    rbase = (rbase + 127) // 128 * 128

    def sb(name, shape, dtype, off):
        uid[0] += 1
        return nc.alloc_sbuf_tensor_at(f"{name}{uid[0]}", list(shape), dtype, offset=rbase + off)

    class SemH:

        def __init__(self, handle):
            self.h = handle
            self.base = 0
            self.pending = 0

        def add(self, n):
            self.pending += n
            return n

    sem_pool = {}

    def sems(*names):
        out = []
        for n in names:
            if n not in sem_pool:
                sem_pool[n] = SemH(nc.alloc_semaphore(f"sem_{n}"))
            out.append(sem_pool[n])
        return out

    def end_phase():
        nc.all_engine_barrier()
        for s in sem_pool.values():
            s.base += s.pending
            s.pending = 0

    class Chain:

        def __init__(self, sem):
            self.sem = sem

        def seal(self, ins):
            ins.then_inc(self.sem.h, self.sem.add(1))
            return ins

        def wait(self, eng):
            eng.wait_ge(self.sem.h, self.sem.base + self.sem.pending)

    xin = nc.dram_tensor("xin", [D, S], F32, kind="ExternalInput").ap()
    a1 = nc.dram_tensor("a1", [NHT, 128, NDT * 128], F32, kind="ExternalInput").ap()
    a3 = nc.dram_tensor("a3", [NHT, 128, NDT * 128], F32, kind="ExternalInput").ap()
    w2t = nc.dram_tensor("w2t", [HF, D], F32, kind="ExternalInput").ap()
    aq = nc.dram_tensor("aq", [16, 128, NDT * 128], F32, kind="ExternalInput").ap()
    ak = nc.dram_tensor("ak", [4, 128, NDT * 128], F32, kind="ExternalInput").ap()
    av = nc.dram_tensor("av", [NDT, 128, 512], F32, kind="ExternalInput").ap()
    ao = nc.dram_tensor("ao", [NDT, 128, NDT * 128], F32, kind="ExternalInput").ap()
    h128_in = nc.dram_tensor("h128", [128, 128], BF16, kind="ExternalInput").ap()
    h64_in = nc.dram_tensor("h64", [64, 64], BF16, kind="ExternalInput").ap()
    cosT_in = nc.dram_tensor("cosT", [128, S], F32, kind="ExternalInput").ap()
    sinS_in = nc.dram_tensor("sinS", [128, S], F32, kind="ExternalInput").ap()
    rotm_in = nc.dram_tensor("rotm", [128, 128], BF16, kind="ExternalInput").ap()
    tril_in = nc.dram_tensor("tril", [128, 128], BF16, kind="ExternalInput").ap()

    u_s = nc.dram_tensor("u_s", [HF, D], BF16, kind="Internal").ap()
    p4 = nc.dram_tensor("p4", [HF, D], BF16, kind="Internal").ap()
    xa = nc.dram_tensor("xa", [D, S], F32, kind="Internal").ap()
    xb = nc.dram_tensor("xb", [D, S], F32, kind="Internal").ap()
    qs = nc.dram_tensor("qs", [16, 128, S], BF16, kind="Internal").ap()
    den_s = nc.dram_tensor("den_s", [128, 512], F32, kind="Internal").ap()
    rc_s = nc.dram_tensor("rc_s", [128, 512], F32, kind="Internal").ap()
    yT = nc.dram_tensor("yT", [D, S], F32, kind="ExternalOutput").ap()

    pall = nc.alloc_psum_tensor("pall", [128, 4096], F32)

    def bank(b):
        return pall[:, 512 * b:512 * (b + 1)]

    ones_col = sb("ones_col", [128, 1], BF16, OFF_CONST)
    ones_row = sb("ones_row", [1, 128], BF16, OFF_CONST + 64)
    h128 = sb("h128c", [128, 128], BF16, OFF_CONST + 384)
    h64 = sb("h64c", [64, 64], BF16, OFF_CONST + 640)
    rotm = sb("rotmc", [128, 128], BF16, OFF_CONST + 768)
    tril = sb("trilc", [128, 128], BF16, OFF_CONST + 1024)

    XNT = sb("xnT", [128, NDT * SH], BF16, OFF_XNT)
    HID = sb("hid", [128, NHT * SH], BF16, OFF_HID)
    W1B = sb("w1b", [128, 2 * 2048], BF16, OFF_W1B)
    W3B = sb("w3b", [128, 2 * 2048], BF16, OFF_W3B)
    TMPS = sb("tmps", [128, 4 * 512], BF16, OFF_TMPS)
    PBUF = sb("pbuf", [128, 2 * 2048], BF16, OFF_PBUF)
    XRES = sb("xres", [128, 8 * 512], F32, OFF_XRES)
    XNEW = sb("xnew", [128, 8 * 512], F32, OFF_XNEW)
    COSS = sb("cosS", [128, S], F32, OFF_COSS)
    SINSS = sb("sinS", [128, S], F32, OFF_SINS)
    KTT = sb("kT", [128, 4 * S], BF16, OFF_KT)
    VAUG = sb("vaug", [128, 16 * 520], BF16, OFF_VAUG)
    WVB = sb("wvb", [128, HF], BF16, OFF_WVB)
    OUTT = sb("outT", [128, NDT * S], BF16, OFF_OUTT)
    XNT2 = sb("xnT2", [128, NDT * SH], BF16, OFF_OUTT)

    LN_STA = sb("ln_stA", [1, SH], F32, OFF_W1B)
    LN_STB = sb("ln_stB", [1, SH], F32, OFF_W1B + 4 * KB)
    LN_STC = sb("ln_stC", [1, SH], F32, OFF_W1B + 8 * KB)
    LN_ST16 = sb("ln_st16", [1, 2 * SH], BF16, OFF_W1B + 12 * KB)
    LN_BB = sb("ln_bb16", [128, 2 * SH], BF16, OFF_W1B + 16 * KB)
    LN_MB = sb("ln_meanB", [128, 2 * SH], BF16, 128 * KB)
    LN_RB = sb("ln_rstdB", [128, 2 * SH], BF16, 132 * KB)
    LN_XTFB = sb("ln_xtfb", [128, 2 * SH], F32, OFF_PBUF)

    def emit_lns_stats(blk, src, sh, gate=None):
        sbl2 = list(sems("sbsa", "sbsb"))
        ssq, smm, svec, ssc2 = sems("ssq2", "smm2", "svec2", "ssc22")
        z = {s: s.base + s.pending for s in (ssq, smm, svec, ssc2, sbl2[0], sbl2[1])}
        svch2, = sems("svch2")
        slot = (sh % 2) * SH

        @blk.gpsimd
        def _(g):
            for dt in range(NDT):
                g.wait_ge(smm.h, z[smm] + (2 * (dt - 1) if dt >= 2 else 0))
                if gate is not None:
                    gate(g, dt)
                g.dma_start(LN_BB[:, SH * (dt % 2):SH * (dt % 2 + 1)],
                            src[128 * dt:128 * (dt + 1), SH * sh:SH * (sh + 1)]
                            ).then_inc(sbl2[dt % 2].h, sbl2[dt % 2].add(16))

        @blk.scalar
        def _(sc):
            for dt in range(NDT):
                sc.wait_ge(smm.h, z[smm] + (2 * dt + 1))
                sc.activation(LN_BB[:, SH * (dt % 2):SH * (dt % 2 + 1)],
                              LN_BB[:, SH * (dt % 2):SH * (dt % 2 + 1)],
                              AF.Square).then_inc(ssq.h, ssq.add(1))
            sc.wait_ge(svec.h, z[svec] + (1))
            sc.activation(LN_STB[:], LN_STB[:], AF.Sqrt).then_inc(ssc2.h, ssc2.add(1))

        @blk.tensor
        def _(t):
            for dt in range(NDT):
                t.wait_ge(sbl2[dt % 2].h, z[sbl2[dt % 2]] + (16 * (dt // 2 + 1)))
                for hf in range(2):
                    ins = t.matmul(bank(6)[32 * hf:32 * hf + 1, :], ones_col[:],
                                   LN_BB[:, SH * (dt % 2) + 512 * hf:SH * (dt % 2) + 512 * (hf + 1)],
                                   start=(dt == 0), stop=(dt == NDT - 1), skip_group_check=True)
                ins.then_inc(smm.h, smm.add(1))
                t.wait_ge(ssq.h, z[ssq] + (dt + 1))
                for hf in range(2):
                    ins = t.matmul(bank(7)[32 * hf:32 * hf + 1, :], ones_col[:],
                                   LN_BB[:, SH * (dt % 2) + 512 * hf:SH * (dt % 2) + 512 * (hf + 1)],
                                   start=(dt == 0), stop=(dt == NDT - 1), skip_group_check=True)
                ins.then_inc(smm.h, smm.add(1))
            t.wait_ge(svec.h, z[svec] + (2))
            t.matmul(bank(6), ones_row[:], LN_ST16[0:1, 0:512], start=True, stop=True)
            ins = t.matmul(bank(7), ones_row[:], LN_ST16[0:1, 512:1024], start=True, stop=True)
            ins.then_inc(smm.h, smm.add(1))
            t.wait_ge(svec.h, z[svec] + (3))
            t.matmul(bank(6), ones_row[:], LN_ST16[0:1, SH:SH + 512], start=True, stop=True)
            ins = t.matmul(bank(7), ones_row[:], LN_ST16[0:1, SH + 512:SH + 1024], start=True, stop=True)
            ins.then_inc(smm.h, smm.add(1))

        @blk.vector
        def _(v):
            ch = Chain(svch2)
            v.wait_ge(smm.h, z[smm] + (2 * NDT))
            inv = 1.0 / float(D)
            for hf in range(2):
                ch.seal(v.tensor_scalar_mul(LN_STA[0:1, 512 * hf:512 * (hf + 1)],
                                            bank(6)[32 * hf:32 * hf + 1, :], inv))
            for hf in range(2):
                ch.seal(v.tensor_scalar_mul(LN_STB[0:1, 512 * hf:512 * (hf + 1)],
                                            bank(7)[32 * hf:32 * hf + 1, :], inv))
            ch.wait(v)
            ch.seal(v.tensor_tensor(LN_STC[:], LN_STA[:], LN_STA[:], op=ALU.mult))
            ch.wait(v)
            ch.seal(v.tensor_tensor(LN_STB[:], LN_STB[:], LN_STC[:], op=ALU.subtract))
            ch.wait(v)
            v.tensor_scalar_add(LN_STB[:], LN_STB[:], EPS).then_inc(svec.h, svec.add(1))
            v.wait_ge(ssc2.h, z[ssc2] + (1))
            ch.seal(v.reciprocal(LN_STC[:], LN_STB[:]))
            v.tensor_copy(LN_ST16[0:1, 0:SH], LN_STA[:])
            ch.wait(v)
            ch.seal(v.tensor_copy(LN_ST16[0:1, SH:2 * SH], LN_STC[:]))
            ch.wait(v)
            v.engine_nop().then_inc(svec.h, svec.add(1))
            v.wait_ge(smm.h, z[smm] + (2 * NDT + 1))
            ch.seal(v.tensor_copy(LN_MB[:, slot:slot + SH], pall[:, 6 * 512:8 * 512]))
            ch.wait(v)
            v.engine_nop().then_inc(svec.h, svec.add(1))
            v.wait_ge(smm.h, z[smm] + (2 * NDT + 2))
            v.tensor_copy(LN_RB[:, slot:slot + SH], pall[:, 6 * 512:8 * 512])

    def emit_lns_apply(blk, src, sh, xnt):
        sxfl = list(sems("sxfa", "sxfb"))
        sap, svch5 = sems("sap", "svch5")
        zx = {s: s.base + s.pending for s in (sap, sxfl[0], sxfl[1])}
        slot = (sh % 2) * SH

        @blk.sync
        def _(s):
            for dt in range(NDT):
                if dt >= 2:
                    s.wait_ge(sap.h, zx[sap] + (dt - 1))
                s.dma_start(LN_XTFB[:, SH * (dt % 2):SH * (dt % 2 + 1)],
                            src[128 * dt:128 * (dt + 1), SH * sh:SH * (sh + 1)]
                            ).then_inc(sxfl[dt % 2].h, sxfl[dt % 2].add(16))

        @blk.vector
        def _(v):
            ch = Chain(svch5)
            for dt in range(NDT):
                v.wait_ge(sxfl[dt % 2].h, zx[sxfl[dt % 2]] + (16 * (dt // 2 + 1)))
                ch.seal(v.tensor_tensor(LN_XTFB[:, SH * (dt % 2):SH * (dt % 2 + 1)],
                                        LN_XTFB[:, SH * (dt % 2):SH * (dt % 2 + 1)],
                                        LN_MB[:, slot:slot + SH], op=ALU.subtract))
                ch.wait(v)
                v.tensor_tensor(xnt[:, SH * dt:SH * (dt + 1)],
                                LN_XTFB[:, SH * (dt % 2):SH * (dt % 2 + 1)],
                                LN_RB[:, slot:slot + SH],
                                op=ALU.mult).then_inc(sap.h, sap.add(1))
        return sap, zx[sap]

    (s_c,) = sems("c")
    with nc.Block() as blk:
        @blk.gpsimd
        def _(g):
            g.dma_start(h128[:], h128_in).then_inc(s_c.h, s_c.add(16))
            g.dma_start(h64[:], h64_in).then_inc(s_c.h, s_c.add(16))
            g.dma_start(rotm[:], rotm_in).then_inc(s_c.h, s_c.add(16))
            g.dma_start(tril[:], tril_in).then_inc(s_c.h, s_c.add(16))
            g.memset(ones_col[:], 1.0)
            g.memset(ones_row[:], 1.0)
            g.wait_ge(s_c.h, s_c.base + (64))
    end_phase()

    s1b = sb("s1b", [128, 2 * D], BF16, OFF_S1B)
    u_sb = sb("u_sb", [128, 2 * D], BF16, OFF_USB)
    s_ld0, s_ld1, s_mm, s_v, s_st0, s_st1 = sems("ld0", "ld1", "mm", "v", "st0", "st1")
    s_ld = [s_ld0, s_ld1]
    s_st = [s_st0, s_st1]
    with nc.Block() as blk:
        @blk.gpsimd
        def _(g):
            for bb in range(NHT):
                if bb >= 2:
                    g.wait_ge(s_mm.h, s_mm.base + (bb - 1))
                g.dma_start(s1b[:, 2048 * (bb % 2):2048 * (bb % 2 + 1)],
                            w2t[128 * bb:128 * (bb + 1), :]).then_inc(s_ld[bb % 2].h, s_ld[bb % 2].add(16))

        @blk.tensor
        def _(t):
            for bb in range(NHT):
                t.wait_ge(s_ld[bb % 2].h, s_ld[bb % 2].base + (16 * (bb // 2 + 1)))
                if bb >= 1:
                    t.wait_ge(s_v.h, s_v.base + (4 * bb))
                for j in range(4):
                    ins = t.matmul(bank(j), h128[:],
                                   s1b[:, 2048 * (bb % 2) + 512 * j:2048 * (bb % 2) + 512 * (j + 1)],
                                   start=True, stop=True)
                ins.then_inc(s_mm.h, s_mm.add(1))

        @blk.vector
        def _(v):
            for bb in range(NHT):
                v.wait_ge(s_mm.h, s_mm.base + (bb + 1))
                if bb >= 2:
                    v.wait_ge(s_st[bb % 2].h, s_st[bb % 2].base + (16 * ((bb - 2) // 2 + 1)))
                v.tensor_scalar_mul(
                    u_sb[:, 2048 * (bb % 2):2048 * (bb % 2) + 2048],
                    pall[:, 0:2048], C_FWHT).then_inc(s_v.h, s_v.add(4))

        @blk.sync
        def _(s):
            for bb in range(NHT):
                s.wait_ge(s_v.h, s_v.base + (4 * (bb + 1)))
                s.dma_start(u_s[128 * bb:128 * (bb + 1), :],
                            u_sb[:, 2048 * (bb % 2):2048 * (bb % 2 + 1)]).then_inc(s_st[bb % 2].h, s_st[bb % 2].add(16))
            s.wait_ge(s_st0.h, s_st0.base + (16 * (NHT // 2)))
            s.wait_ge(s_st1.h, s_st1.base + (16 * (NHT // 2)))
    end_phase()

    if stages == "pb1":
        stages = "pbuild"
        skip_stage2 = True
    else:
        skip_stage2 = False
    v2b = sb("v2b", [64, 2 * D], BF16, OFF_V2B)
    p2sb = sb("p2sb", [64, 2 * D], BF16, OFF_P2SB)
    u_v = u_s.rearrange("(b p) d -> p b d", p=128)
    s_ld0, s_ld1, s_mm, s_v, s_st0, s_st1 = sems("ld2a", "ld2b", "mm", "v", "st2a", "st2b")
    s_ld = [s_ld0, s_ld1]
    s_st = [s_st0, s_st1]
    if skip_stage2:
        with nc.Block() as blk: pass
    with nc.Block() as blk:
        @blk.sync
        def _(s):
            for p in range(0 if skip_stage2 else 128):
                if p >= 2:
                    s.wait_ge(s_mm.h, s_mm.base + (p - 1))
                s.dma_start(v2b[:, 2048 * (p % 2):2048 * (p % 2 + 1)], u_v[p]).then_inc(s_ld[p % 2].h, s_ld[p % 2].add(16))

        @blk.tensor
        def _(t):
            for p in range(0 if skip_stage2 else 128):
                t.wait_ge(s_ld[p % 2].h, s_ld[p % 2].base + (16 * (p // 2 + 1)))
                if p >= 1:
                    t.wait_ge(s_v.h, s_v.base + (4 * p))
                for j in range(4):
                    ins = t.matmul(bank(j)[0:64, :], h64[:],
                                   v2b[:, 2048 * (p % 2) + 512 * j:2048 * (p % 2) + 512 * (j + 1)],
                                   start=True, stop=True)
                ins.then_inc(s_mm.h, s_mm.add(1))

        @blk.vector
        def _(v):
            for p in range(0 if skip_stage2 else 128):
                v.wait_ge(s_mm.h, s_mm.base + (p + 1))
                if p >= 2:
                    v.wait_ge(s_st[p % 2].h, s_st[p % 2].base + (16 * ((p - 2) // 2 + 1)))
                v.tensor_copy(
                    p2sb[:, 2048 * (p % 2):2048 * (p % 2) + 2048].rearrange(
                        "p (j k) -> p j k", j=4),
                    pall[0:64, 0:2048].rearrange("p (j k) -> p j k", j=4)
                ).then_inc(s_v.h, s_v.add(4))

        @blk.scalar
        def _(sc):
            for p in range(0 if skip_stage2 else 128):
                sc.wait_ge(s_v.h, s_v.base + (4 * (p + 1)))
                sc.dma_start(p4[64 * p:64 * (p + 1), :],
                             p2sb[:, 2048 * (p % 2):2048 * (p % 2 + 1)]).then_inc(s_st[p % 2].h, s_st[p % 2].add(16))
            if not skip_stage2:
                sc.wait_ge(s_st0.h, s_st0.base + (16 * 64))
                sc.wait_ge(s_st1.h, s_st1.base + (16 * 64))
        if stages != "pbuild":
            emit_lns_stats(blk, xin, 0)
    end_phase()

    def emit_up(blk, sh, xnt, apw=None):
        sw1 = list(sems("sw1a", "sw1b"))
        sw3 = list(sems("sw3a", "sw3b"))
        smg, sml, ssl, stt = sems("smg", "sml", "ssl", "stt")
        if True:
            @blk.gpsimd
            def _(g):
                for ht in range(NHT):
                    sl = 2048 * (ht % 2)
                    if ht >= 2:
                        g.wait_ge(smg.h, smg.base + (ht - 1))
                    g.dma_start(W1B[:, sl:sl + 2048], a1[ht]).then_inc(sw1[ht % 2].h, sw1[ht % 2].add(16))
                    if ht >= 2:
                        g.wait_ge(sml.h, sml.base + (ht - 1))
                    g.dma_start(W3B[:, sl:sl + 2048], a3[ht]).then_inc(sw3[ht % 2].h, sw3[ht % 2].add(16))

            @blk.tensor
            def _(t):
                for ht in range(NHT):
                    sl = 2048 * (ht % 2)
                    t.wait_ge(sw1[ht % 2].h, sw1[ht % 2].base + (16 * (ht // 2 + 1)))
                    if ht >= 1:
                        t.wait_ge(ssl.h, ssl.base + (2 * ht))
                    for dt in range(NDT):
                        if ht == 0 and apw is not None:
                            t.wait_ge(apw[0].h, apw[1] + (dt + 1))
                        for hf in range(2):
                            ins = t.matmul(bank(hf),
                                           W1B[:, sl + 128 * dt:sl + 128 * (dt + 1)],
                                           xnt[:, SH * dt + 512 * hf:SH * dt + 512 * (hf + 1)],
                                           start=(dt == 0), stop=(dt == NDT - 1),
                                           skip_group_check=True)
                    ins.then_inc(smg.h, smg.add(1))
                    t.wait_ge(sw3[ht % 2].h, sw3[ht % 2].base + (16 * (ht // 2 + 1)))
                    if ht >= 1:
                        t.wait_ge(stt.h, stt.base + (2 * ht))
                    for dt in range(NDT):
                        for hf in range(2):
                            ins = t.matmul(bank(2 + hf),
                                           W3B[:, sl + 128 * dt:sl + 128 * (dt + 1)],
                                           xnt[:, SH * dt + 512 * hf:SH * dt + 512 * (hf + 1)],
                                           start=(dt == 0), stop=(dt == NDT - 1),
                                           skip_group_check=True)
                    ins.then_inc(sml.h, sml.add(1))

            @blk.scalar
            def _(sc):
                for ht in range(NHT):
                    sc.wait_ge(smg.h, smg.base + (ht + 1))
                    if ht >= 2:
                        sc.wait_ge(stt.h, stt.base + (2 * (ht - 1)))
                    for hf in range(2):
                        sc.activation(
                            TMPS[:, 1024 * (ht % 2) + 512 * hf:1024 * (ht % 2) + 512 * (hf + 1)],
                            bank(hf), AF.Silu).then_inc(ssl.h, ssl.add(1))

            @blk.vector
            def _(v):
                for ht in range(NHT):
                    v.wait_ge(ssl.h, ssl.base + (2 * (ht + 1)))
                    v.wait_ge(sml.h, sml.base + (ht + 1))
                    for hf in range(2):
                        v.tensor_tensor(
                            HID[:, SH * ht + 512 * hf:SH * ht + 512 * (hf + 1)],
                            TMPS[:, 1024 * (ht % 2) + 512 * hf:1024 * (ht % 2) + 512 * (hf + 1)],
                            bank(2 + hf), op=ALU.mult).then_inc(stt.h, stt.add(1))

    def emit_down(blk, src, dst, sh):
        spl = list(sems("spa", "spb"))
        sxrl = list(sems("sxra", "sxrb"))
        sstl = list(sems("ssta", "sstb"))
        smd, svd = sems("smd", "svd")
        zst = [s.base + s.pending for s in sstl]
        if True:
            @blk.sync
            def _(s):
                for dg in range(8):
                    for hb in range(8):
                        it = 8 * dg + hb
                        if it >= 2:
                            s.wait_ge(smd.h, smd.base + (it - 1))
                        srcv = p4[1024 * hb:1024 * (hb + 1), 256 * dg:256 * (dg + 1)] \
                            .rearrange("(h p) d -> p h d", p=128)
                        s.dma_start(
                            PBUF[:, 2048 * (it % 2):2048 * (it % 2 + 1)].rearrange(
                                "p (h d) -> p h d", h=8), srcv).then_inc(spl[it % 2].h, spl[it % 2].add(16))

            @blk.scalar
            def _(sc):
                for dg in range(8):
                    if dg >= 2:
                        sc.wait_ge(svd.h, svd.base + (4 * (dg - 1)))
                    for tt in range(4):
                        dc, hf = tt // 2, tt % 2
                        dch = 2 * dg + dc
                        sc.dma_start(
                            XRES[:, 512 * (4 * (dg % 2) + tt):512 * (4 * (dg % 2) + tt + 1)],
                            src[128 * dch:128 * (dch + 1),
                                SH * sh + 512 * hf:SH * sh + 512 * (hf + 1)]).then_inc(sxrl[dg % 2].h, sxrl[dg % 2].add(16))
                    if dg >= 1:
                        for tt in range(4):
                            dc, hf = tt // 2, tt % 2
                            dch = 2 * (dg - 1) + dc
                            sc.wait_ge(svd.h, svd.base + (4 * (dg - 1) + tt + 1))
                            sc.dma_start(
                                dst[128 * dch:128 * (dch + 1),
                                    SH * sh + 512 * hf:SH * sh + 512 * (hf + 1)],
                                XNEW[:, 512 * (4 * ((dg - 1) % 2) + tt):512 * (4 * ((dg - 1) % 2) + tt + 1)]
                            ).then_inc(sstl[(dg - 1) % 2].h, sstl[(dg - 1) % 2].add(16))
                for tt in range(4):
                    dc, hf = tt // 2, tt % 2
                    dch = 14 + dc
                    sc.wait_ge(svd.h, svd.base + (28 + tt + 1))
                    sc.dma_start(
                        dst[128 * dch:128 * (dch + 1),
                            SH * sh + 512 * hf:SH * sh + 512 * (hf + 1)],
                        XNEW[:, 512 * (4 + tt):512 * (4 + tt + 1)]).then_inc(sstl[1].h, sstl[1].add(16))
                sc.wait_ge(sstl[0].h, sstl[0].base + (16 * 16))
                sc.wait_ge(sstl[1].h, sstl[1].base + (16 * 16))

            @blk.tensor
            def _(t):
                for dg in range(8):
                    if dg >= 1:
                        t.wait_ge(svd.h, svd.base + (4 * dg))
                    for hb in range(8):
                        it = 8 * dg + hb
                        t.wait_ge(spl[it % 2].h, spl[it % 2].base + (16 * (it // 2 + 1)))
                        for hl in range(8):
                            ht = 8 * hb + hl
                            for tt in range(4):
                                dc, hf = tt // 2, tt % 2
                                ins = t.matmul(
                                    bank(tt),
                                    PBUF[:, 2048 * (it % 2) + 256 * hl + 128 * dc:
                                         2048 * (it % 2) + 256 * hl + 128 * (dc + 1)],
                                    HID[:, SH * ht + 512 * hf:SH * ht + 512 * (hf + 1)],
                                    start=(ht == 0), stop=(ht == NHT - 1),
                                    skip_group_check=True)
                        ins.then_inc(smd.h, smd.add(1))

            @blk.vector
            def _(v):
                for dg in range(8):
                    v.wait_ge(smd.h, smd.base + (8 * dg + 8))
                    v.wait_ge(sxrl[dg % 2].h, sxrl[dg % 2].base + (64 * (dg // 2 + 1)))
                    if dg >= 2:
                        v.wait_ge(sstl[dg % 2].h, sstl[dg % 2].base + (64 * ((dg - 2) // 2 + 1)))
                    for tt in range(4):
                        v.tensor_tensor(
                            XNEW[:, 512 * (4 * (dg % 2) + tt):512 * (4 * (dg % 2) + tt + 1)],
                            XRES[:, 512 * (4 * (dg % 2) + tt):512 * (4 * (dg % 2) + tt + 1)],
                            bank(tt), op=ALU.add).then_inc(svd.h, svd.add(1))
        return sstl, zst


    def emit_rope_consts():
        (s_c2,) = sems("c2")
        with nc.Block() as blk:
            @blk.sync
            def _(s):
                s.dma_start(COSS[:], cosT_in).then_inc(s_c2.h, s_c2.add(16))
                s.dma_start(SINSS[:], sinS_in).then_inc(s_c2.h, s_c2.add(16))
                s.wait_ge(s_c2.h, s_c2.base + (32))
        end_phase()

    def emit_qkv(blk, sh, xnt, apw=None):
        QRAW = sb("qraw", [128, 2 * 512], BF16, OFF_QRAW)
        QP = sb("qp", [128, 2 * 512], BF16, OFF_QP)
        T1 = sb("t1", [128, 2 * 512], F32, OFF_T1)
        T2 = sb("t2", [128, 2 * 512], F32, OFF_T2)

        tiles = [("q", ct, hf) for ct in range(16) for hf in range(2)] + \
                [("k", ct, hf) for ct in range(4) for hf in range(2)]

        swql = list(sems("swqa", "swqb"))
        sqsl = list(sems("sqsa", "sqsb"))
        smq, smr, sv1, sv2, sv3, smv, svv, swv, svch3 = sems(
            "smq", "smr", "sv1", "sv2", "sv3", "smv", "svv", "swv", "svch3")

        if True:
            @blk.gpsimd
            def _(g):
                for ci in range(20):
                    if ci >= 2:
                        g.wait_ge(smq.h, smq.base + (2 * (ci - 1)))
                    srcw = aq[ci] if ci < 16 else ak[ci - 16]
                    g.dma_start(W1B[:, 2048 * (ci % 2):2048 * (ci % 2 + 1)], srcw).then_inc(swql[ci % 2].h, swql[ci % 2].add(16))
                for dt in range(NDT):
                    g.dma_start(WVB[:, 512 * dt:512 * (dt + 1)], av[dt]).then_inc(swv.h, swv.add(16))

            @blk.tensor
            def _(t):
                def proj_mm(i):
                    kind, ct, hf = tiles[i]
                    ci = i // 2
                    par = i % 2
                    t.wait_ge(swql[ci % 2].h, swql[ci % 2].base + (16 * (ci // 2 + 1)))
                    if i >= 2:
                        t.wait_ge(sv1.h, sv1.base + (i - 1))
                    for dt in range(NDT):
                        if i == 0 and apw is not None:
                            t.wait_ge(apw[0].h, apw[1] + (dt + 1))
                        ins = t.matmul(bank(par),
                                       W1B[:, 2048 * (ci % 2) + 128 * dt:2048 * (ci % 2) + 128 * (dt + 1)],
                                       xnt[:, SH * dt + 512 * hf:SH * dt + 512 * (hf + 1)],
                                       start=(dt == 0), stop=(dt == NDT - 1),
                                       skip_group_check=True)
                    ins.then_inc(smq.h, smq.add(1))

                proj_mm(0)
                for i, (kind, ct, hf) in enumerate(tiles):
                    par = i % 2
                    if i + 1 < len(tiles):
                        proj_mm(i + 1)
                    t.wait_ge(sv1.h, sv1.base + (i + 1))
                    if i >= 2:
                        t.wait_ge(sv2.h, sv2.base + (i - 1))
                    t.matmul(bank(2 + par), rotm[:],
                             QRAW[:, 512 * par:512 * (par + 1)],
                             start=True, stop=True).then_inc(smr.h, smr.add(1))
                t.wait_ge(swv.h, swv.base + (16 * NDT))
                for st in range(8):
                    if st >= 2:
                        t.wait_ge(svv.h, svv.base + (st - 1))
                    for dt in range(NDT):
                        ins = t.matmul(bank(4 + st % 2),
                                       xnt[:, SH * dt + 128 * st:SH * dt + 128 * (st + 1)],
                                       WVB[:, 512 * dt:512 * (dt + 1)],
                                       start=(dt == 0), stop=(dt == NDT - 1),
                                       skip_group_check=True)
                    ins.then_inc(smv.h, smv.add(1))

            @blk.vector
            def _(v):
                ch = Chain(svch3)
                for i, (kind, ct, hf) in enumerate(tiles):
                    par = i % 2
                    so = SH * sh + 512 * hf
                    v.wait_ge(smq.h, smq.base + (i + 1))
                    if i >= 2:
                        v.wait_ge(smr.h, smr.base + (i - 1))
                    ch.seal(v.tensor_copy(QRAW[:, 512 * par:512 * (par + 1)], bank(par)))
                    ch.wait(v)
                    v.engine_nop().then_inc(sv1.h, sv1.add(1))
                    ch.seal(v.tensor_tensor(T1[:, 512 * par:512 * (par + 1)],
                                            QRAW[:, 512 * par:512 * (par + 1)],
                                            COSS[:, so:so + 512], op=ALU.mult))
                    v.wait_ge(smr.h, smr.base + (i + 1))
                    ch.seal(v.tensor_tensor(T2[:, 512 * par:512 * (par + 1)], bank(2 + par),
                                            SINSS[:, so:so + 512], op=ALU.mult))
                    ch.wait(v)
                    v.engine_nop().then_inc(sv2.h, sv2.add(1))
                    if kind == "q":
                        if i >= 2:
                            v.wait_ge(sqsl[i % 2].h, sqsl[i % 2].base + (16 * ((i - 2) // 2 + 1)))
                        v.tensor_tensor(QP[:, 512 * par:512 * (par + 1)],
                                        T1[:, 512 * par:512 * (par + 1)],
                                        T2[:, 512 * par:512 * (par + 1)], op=ALU.add).then_inc(sv3.h, sv3.add(1))
                    else:
                        v.tensor_tensor(KTT[:, S * ct + so:S * ct + so + 512],
                                        T1[:, 512 * par:512 * (par + 1)],
                                        T2[:, 512 * par:512 * (par + 1)], op=ALU.add).then_inc(sv3.h, sv3.add(1))
                for st in range(8):
                    stg = 8 * sh + st
                    v.wait_ge(smv.h, smv.base + (st + 1))
                    ch.seal(v.memset(VAUG[:, 520 * stg:520 * (stg + 1)], 1.0))
                    ch.wait(v)
                    v.tensor_copy(
                        VAUG[:, 520 * stg:520 * (stg + 1)].rearrange(
                            "p (g k) -> p g k", g=8, k=65)[:, :, 0:64],
                        bank(4 + st % 2).rearrange("p (g k) -> p g k", g=8),
                    ).then_inc(svv.h, svv.add(1))

            @blk.sync
            def _(s):
                for i, (kind, ct, hf) in enumerate(tiles):
                    if kind != "q":
                        continue
                    par = i % 2
                    s.wait_ge(sv3.h, sv3.base + (i + 1))
                    s.dma_start(qs[ct][:, SH * sh + 512 * hf:SH * sh + 512 * (hf + 1)],
                                QP[:, 512 * par:512 * (par + 1)]).then_inc(sqsl[i % 2].h, sqsl[i % 2].add(16))
                s.wait_ge(sqsl[0].h, sqsl[0].base + (16 * 16))
                s.wait_ge(sqsl[1].h, sqsl[1].base + (16 * 16))

    def emit_scores():
        QBUF = sb("qbuf", [128, 2 * 512], BF16, OFF_QBUF)
        PRB = sb("prb", [128, 4 * 512], BF16, OFF_PRB)
        DENS = sb("dens", [128, 2 * 1024], F32, OFF_T1)

        sqll = list(sems("sqla", "sqlb"))
        sdsl = list(sems("sdsa", "sdsb"))
        sms, sse, smk, smav, sout, sdnv = sems(
            "sms", "sse", "smk", "smav", "sout", "sdnv")

        its = [(c, dkt, j) for c in range(4) for dkt in range(4) for j in range(4)]
        NIT = len(its)

        sms_h, sse_h, smk_h = {}, {}, {}
        av_cum = [0] * (NIT + 1)
        c_sms = c_sse = c_smk = 0
        for idx, (c, dkt, j) in enumerate(its):
            nkt = 4 * c + 4
            for kt in range(nkt):
                c_sms += 1
                sms_h[(idx, kt)] = c_sms
                c_sse += 1
                sse_h[(idx, kt)] = c_sse
                if kt >= 4 * c:
                    c_smk += 1
                smk_h[(idx, kt)] = c_smk
            av_cum[idx + 1] = av_cum[idx] + 2 * nkt
        sms_cum = {idx: sms_h[(idx, 4 * its[idx][0] + 3)] for idx in range(NIT)}

        def prb_ap(par, off):
            return PRB[:, 512 * (2 * par):512 * (2 * par) + 1024].rearrange(
                "p (b k) -> p b k", b=2)[:, :, off:512]

        def sco_ap(par, off):
            return pall[:, 512 * (2 * par):512 * (2 * par) + 1024].rearrange(
                "p (b k) -> p b k", b=2)[:, :, off:512]

        with nc.Block() as blk:
            @blk.sync
            def _(s):
                for idx, (c, dkt, j) in enumerate(its):
                    hA, hB = 8 * dkt + j, 8 * dkt + 4 + j
                    if idx >= 2:
                        s.wait_ge(sms.h, sms.base + (sms_cum[idx - 2]))
                    qb = 512 * (idx % 2)
                    for which, h in ((0, hA), (1, hB)):
                        s.dma_start(QBUF[64 * which:64 * (which + 1), qb:qb + 512],
                                    qs[h // 2][64 * (h % 2):64 * (h % 2) + 64, 512 * c:512 * (c + 1)]
                                    ).then_inc(sqll[idx % 2].h, sqll[idx % 2].add(16))
                    if idx >= 1:
                        s.wait_ge(sdnv.h, sdnv.base + (idx))
                        jp = (idx - 1) % 2
                        for w in range(2):
                            s.dma_start(den_s[2 * (idx - 1) + w:2 * (idx - 1) + w + 1, :],
                                        DENS[64:65, 1024 * jp + 512 * w:1024 * jp + 512 * (w + 1)]
                                        ).then_inc(sdsl[jp].h, sdsl[jp].add(16))
                s.wait_ge(sdnv.h, sdnv.base + (NIT))
                jp = (NIT - 1) % 2
                for w in range(2):
                    s.dma_start(den_s[2 * (NIT - 1) + w:2 * (NIT - 1) + w + 1, :],
                                DENS[64:65, 1024 * jp + 512 * w:1024 * jp + 512 * (w + 1)]
                                ).then_inc(sdsl[jp].h, sdsl[jp].add(16))
                s.wait_ge(sdsl[0].h, sdsl[0].base + (16 * 64))
                s.wait_ge(sdsl[1].h, sdsl[1].base + (16 * 64))

            @blk.tensor
            def _(t):
                for idx, (c, dkt, j) in enumerate(its):
                    nkt = 4 * c + 4
                    gA, gB = (8 * dkt + j) // 4, (8 * dkt + 4 + j) // 4
                    ip = idx % 2
                    qb = 512 * ip
                    t.wait_ge(sqll[ip].h, sqll[ip].base + (32 * (idx // 2 + 1)))
                    if idx >= 2:
                        t.wait_ge(sout.h, sout.base + (2 * (idx - 1)))
                    def score_mm(kt):
                        off = 128 * (kt - 4 * c) if kt >= 4 * c else 0
                        par = kt % 2
                        if kt >= 2:
                            t.wait_ge(sse.h, sse.base + (sse_h[(idx, kt - 2)]))
                        elif idx >= 1:
                            t.wait_ge(sse.h, sse.base + (sse_h[(idx - 1, 4 * its[idx - 1][0] + 3)]))
                        t.matmul(bank(2 * par)[:, off:512],
                                 KTT[0:64, S * dkt + 128 * kt:S * dkt + 128 * (kt + 1)],
                                 QBUF[0:64, qb + off:qb + 512],
                                 start=True, stop=True, tile_position=(0, 0))
                        ins = t.matmul(bank(2 * par + 1)[:, off:512],
                                       KTT[64:128, S * dkt + 128 * kt:S * dkt + 128 * (kt + 1)],
                                       QBUF[64:128, qb + off:qb + 512],
                                       start=True, stop=True, tile_position=(64, 0))
                        ins.then_inc(sms.h, sms.add(1))

                    score_mm(0)
                    for kt in range(nkt):
                        off = 128 * (kt - 4 * c) if kt >= 4 * c else 0
                        par = kt % 2
                        if kt + 1 < nkt:
                            score_mm(kt + 1)
                        t.wait_ge(sse.h, sse.base + (sse_h[(idx, kt)]))
                        if kt >= 4 * c:
                            t.wait_ge(smk.h, smk.base + (smk_h[(idx, kt)]))
                        t.matmul(bank(4 + 2 * ip)[0:65, off:512],
                                 VAUG[:, 520 * kt + 65 * gA:520 * kt + 65 * gA + 65],
                                 PRB[:, 512 * (2 * par) + off:512 * (2 * par) + 512],
                                 start=(kt == 0), stop=(kt == nkt - 1),
                                 skip_group_check=True).then_inc(smav.h, smav.add(1))
                        t.matmul(bank(5 + 2 * ip)[0:65, off:512],
                                 VAUG[:, 520 * kt + 65 * gB:520 * kt + 65 * gB + 65],
                                 PRB[:, 512 * (2 * par + 1) + off:512 * (2 * par + 1) + 512],
                                 start=(kt == 0), stop=(kt == nkt - 1),
                                 skip_group_check=True).then_inc(smav.h, smav.add(1))

            @blk.scalar
            def _(sc):
                for idx, (c, dkt, j) in enumerate(its):
                    nkt = 4 * c + 4
                    for kt in range(nkt):
                        off = 128 * (kt - 4 * c) if kt >= 4 * c else 0
                        par = kt % 2
                        sc.wait_ge(sms.h, sms.base + (sms_h[(idx, kt)]))
                        if kt >= 2:
                            sc.wait_ge(smav.h, smav.base + (av_cum[idx] + 2 * (kt - 1)))
                        elif idx >= 1:
                            sc.wait_ge(smav.h, smav.base + (av_cum[idx]))
                        sc.activation(prb_ap(par, off), sco_ap(par, off),
                                      AF.Exp, scale=0.125).then_inc(sse.h, sse.add(1))

            @blk.vector
            def _(v):
                trib = tril[:].rearrange("p (o k) -> p o k", o=1).broadcast_to((128, 2, 128))
                for idx, (c, dkt, j) in enumerate(its):
                    hA, hB = 8 * dkt + j, 8 * dkt + 4 + j
                    nkt = 4 * c + 4
                    ip = idx % 2
                    for kt in range(4 * c, nkt):
                        off = 128 * (kt - 4 * c)
                        par = kt % 2
                        v.wait_ge(sse.h, sse.base + (sse_h[(idx, kt)]))
                        v.tensor_tensor(prb_ap(par, off)[:, :, 0:128],
                                        prb_ap(par, off)[:, :, 0:128],
                                        trib, op=ALU.mult).then_inc(smk.h, smk.add(1))
                    v.wait_ge(smav.h, smav.base + (av_cum[idx + 1]))
                    if idx >= 2:
                        v.wait_ge(sdsl[ip].h, sdsl[ip].base + (32 * ((idx - 2) // 2 + 1)))
                    for which, h in ((0, hA), (1, hB)):
                        v.tensor_copy(
                            OUTT[64 * (h % 2):64 * (h % 2) + 64,
                                 S * (h // 2) + 512 * c:S * (h // 2) + 512 * (c + 1)],
                            bank(4 + which + 2 * ip)[0:64, :]).then_inc(sout.h, sout.add(1))
                    v.tensor_copy(
                        DENS[64:65, 1024 * ip:1024 * ip + 1024].rearrange("p (b k) -> p b k", b=2),
                        pall[:, 512 * (4 + 2 * ip):512 * (6 + 2 * ip)].rearrange(
                            "p (b k) -> p b k", b=2)[64:65, :, :],
                    ).then_inc(sdnv.h, sdnv.add(1))
        end_phase()

    def emit_norm():
        DENB = sb("denb", [128, 512], F32, OFF_T1)
        RCA = sb("rca", [128, 512], F32, OFF_T1 + 2 * KB)
        RCB = sb("rcb", [128, 4 * 512], F32, OFF_TMPS)

        its = [(c, dkt, j) for c in range(4) for dkt in range(4) for j in range(4)]
        sdl, svr, srw, svt = sems("sdl", "svr", "srw", "svt")
        srbl = list(sems("srb0", "srb1", "srb2", "srb3"))

        with nc.Block() as blk:
            @blk.sync
            def _(s):
                s.dma_start(DENB[:], den_s[:, :]).then_inc(sdl.h, sdl.add(16))
                s.wait_ge(svr.h, svr.base + (1))
                s.dma_start(rc_s[:, :], RCA[:]).then_inc(srw.h, srw.add(16))
                s.wait_ge(srw.h, srw.base + (16))
                for r in range(128):
                    if r >= 4:
                        s.wait_ge(svt.h, svt.base + (r - 3))
                    s.dma_start(RCB[:, 512 * (r % 4):512 * (r % 4 + 1)],
                                rc_s[r:r + 1, :].broadcast_to((128, 512))
                                ).then_inc(srbl[r % 4].h, srbl[r % 4].add(16))

            @blk.vector
            def _(v):
                v.wait_ge(sdl.h, sdl.base + (16))
                v.reciprocal(RCA[:], DENB[:]).then_inc(svr.h, svr.add(1))
                for r in range(128):
                    idx, which = r // 2, r % 2
                    c, dkt, j = its[idx]
                    h = 8 * dkt + j + 4 * which
                    v.wait_ge(srbl[r % 4].h, srbl[r % 4].base + (16 * (r // 4 + 1)))
                    hp = 64 * (h % 2)
                    sl = OUTT[hp:hp + 64,
                              S * (h // 2) + 512 * c:S * (h // 2) + 512 * (c + 1)]
                    v.tensor_tensor(sl, sl, RCB[hp:hp + 64, 512 * (r % 4):512 * (r % 4 + 1)],
                                    op=ALU.mult).then_inc(svt.h, svt.add(1))
        end_phase()

    def emit_outproj(blk, src, dst):
        swol = list(sems("swoa", "swob"))
        sxr2l = list(sems("sxr2a", "sxr2b"))
        sst2l = list(sems("sst2a", "sst2b"))
        smo, svo = sems("smo", "svo")
        if True:
            @blk.gpsimd
            def _(g):
                for dch in range(NDT):
                    if dch >= 2:
                        g.wait_ge(smo.h, smo.base + (dch - 1))
                    g.dma_start(PBUF[:, 2048 * (dch % 2):2048 * (dch % 2 + 1)],
                                ao[dch]).then_inc(swol[dch % 2].h, swol[dch % 2].add(16))

            @blk.scalar
            def _(sc):
                for dch in range(NDT):
                    if dch >= 2:
                        sc.wait_ge(svo.h, svo.base + (4 * (dch - 1)))
                    for hf in range(4):
                        sc.dma_start(
                            XRES[:, 512 * (4 * (dch % 2) + hf):512 * (4 * (dch % 2) + hf + 1)],
                            src[128 * dch:128 * (dch + 1), 512 * hf:512 * (hf + 1)]).then_inc(sxr2l[dch % 2].h, sxr2l[dch % 2].add(16))
                    if dch >= 1:
                        for hf in range(4):
                            sc.wait_ge(svo.h, svo.base + (4 * (dch - 1) + hf + 1))
                            sc.dma_start(
                                dst[128 * (dch - 1):128 * dch, 512 * hf:512 * (hf + 1)],
                                XNEW[:, 512 * (4 * ((dch - 1) % 2) + hf):512 * (4 * ((dch - 1) % 2) + hf + 1)]
                            ).then_inc(sst2l[(dch - 1) % 2].h, sst2l[(dch - 1) % 2].add(16))
                for hf in range(4):
                    sc.wait_ge(svo.h, svo.base + (60 + hf + 1))
                    sc.dma_start(dst[128 * 15:2048, 512 * hf:512 * (hf + 1)],
                                 XNEW[:, 512 * (4 + hf):512 * (4 + hf + 1)]).then_inc(sst2l[1].h, sst2l[1].add(16))
                sc.wait_ge(sst2l[0].h, sst2l[0].base + (16 * 32))
                sc.wait_ge(sst2l[1].h, sst2l[1].base + (16 * 32))

            @blk.tensor
            def _(t):
                for dch in range(NDT):
                    t.wait_ge(swol[dch % 2].h, swol[dch % 2].base + (16 * (dch // 2 + 1)))
                    if dch >= 1:
                        t.wait_ge(svo.h, svo.base + (4 * dch))
                    for dvt in range(NDT):
                        for hf in range(4):
                            ins = t.matmul(
                                bank(hf),
                                PBUF[:, 2048 * (dch % 2) + 128 * dvt:2048 * (dch % 2) + 128 * (dvt + 1)],
                                OUTT[:, S * dvt + 512 * hf:S * dvt + 512 * (hf + 1)],
                                start=(dvt == 0), stop=(dvt == NDT - 1),
                                skip_group_check=True)
                    ins.then_inc(smo.h, smo.add(1))

            @blk.vector
            def _(v):
                for dch in range(NDT):
                    v.wait_ge(smo.h, smo.base + (dch + 1))
                    v.wait_ge(sxr2l[dch % 2].h, sxr2l[dch % 2].base + (64 * (dch // 2 + 1)))
                    if dch >= 2:
                        v.wait_ge(sst2l[dch % 2].h, sst2l[dch % 2].base + (64 * ((dch - 2) // 2 + 1)))
                    for hf in range(4):
                        v.tensor_tensor(
                            XNEW[:, 512 * (4 * (dch % 2) + hf):512 * (4 * (dch % 2) + hf + 1)],
                            XRES[:, 512 * (4 * (dch % 2) + hf):512 * (4 * (dch % 2) + hf + 1)],
                            bank(hf), op=ALU.add).then_inc(svo.h, svo.add(1))

    if stages != "pbuild":
        with nc.Block() as blk:
            apw = emit_lns_apply(blk, xin, 0, XNT)
            emit_up(blk, 0, XNT, apw)
        end_phase()
        with nc.Block() as blk:
            emit_down(blk, xin, xa, 0)
            emit_lns_stats(blk, xin, 1)
        end_phase()
        with nc.Block() as blk:
            apw = emit_lns_apply(blk, xin, 1, XNT)
            emit_up(blk, 1, XNT, apw)
        end_phase()
        with nc.Block() as blk:
            dn_sstl, dn_zst = emit_down(blk, xin, xa, 1)
            if stages != "ffn1":
                emit_lns_stats(blk, xa, 0)

                def _gate(g, dt, _s=dn_sstl, _z=dn_zst):
                    dgp = dt // 2
                    g.wait_ge(_s[dgp % 2].h, _z[dgp % 2] + 64 * (dgp // 2 + 1))
                emit_lns_stats(blk, xa, 1, gate=_gate)
        end_phase()
        if stages != "ffn1":
            emit_rope_consts()
            with nc.Block() as blk:
                apw = emit_lns_apply(blk, xa, 0, XNT)
                emit_qkv(blk, 0, XNT, apw)
            end_phase()
            with nc.Block() as blk:
                apw = emit_lns_apply(blk, xa, 1, XNT2)
                emit_qkv(blk, 1, XNT2, apw)
            end_phase()
            emit_scores()
            emit_norm()
            with nc.Block() as blk:
                emit_outproj(blk, xa, xb)
            end_phase()
            if stages != "attn":
                with nc.Block() as blk:
                    emit_lns_stats(blk, xb, 0)
                end_phase()
                with nc.Block() as blk:
                    apw = emit_lns_apply(blk, xb, 0, XNT)
                    emit_up(blk, 0, XNT, apw)
                end_phase()
                with nc.Block() as blk:
                    emit_down(blk, xb, yT, 0)
                    emit_lns_stats(blk, xb, 1)
                end_phase()
                with nc.Block() as blk:
                    apw = emit_lns_apply(blk, xb, 1, XNT)
                    emit_up(blk, 1, XNT, apw)
                end_phase()
                with nc.Block() as blk:
                    emit_down(blk, xb, yT, 1)
                end_phase()

    if debug_taps:
        (s_dt,) = sems("dt")
        taps = {"p4": (p4, [HF, D], BF16), "u_s": (u_s, [HF, D], BF16),
                "xa": (xa, [D, S], F32), "xb": (xb, [D, S], F32),
                "qs": (qs, [16, 128, S], BF16),
                "outT": (OUTT[:], [128, NDT * S], BF16),
                "kT": (KTT[:], [128, 4 * S], BF16),
                "vaug": (VAUG[:], [128, 16 * 520], BF16)}
        with nc.Block() as blk:
            @blk.sync
            def _(s):
                n = 0
                for name in debug_taps:
                    src_ap, shp, dt_ = taps[name]
                    dstp = nc.dram_tensor(f"tap_{name}", shp, dt_, kind="ExternalOutput").ap()
                    s.dma_start(dstp, src_ap).then_inc(s_dt.h, s_dt.add(16))
                    n += 1
                s.wait_ge(s_dt.h, s_dt.base + (16 * n))
        end_phase()

    nc.finalize()
    return nc


def make_in_maps(inputs):
    hs = np.asarray(inputs["L0_attn_hidden_states"])
    w = host_prepare(inputs)
    in_maps = []
    for b in range(B):
        m = {"xin": np.ascontiguousarray(hs[b].T)}
        for k in ("a1", "a3", "w2t", "aq", "ak", "av", "ao",
                  "h128", "h64", "cosT", "sinS", "rotm", "tril"):
            m[k] = w[k]
        in_maps.append(m)
    return in_maps


def kernel(**inputs):
    nc = build_nc()
    in_maps = make_in_maps(inputs)
    res = run_bass_kernel_spmd(nc, in_maps, core_ids=list(range(B)))
    out = np.empty((B, S, D), np.float32)
    for b in range(B):
        out[b] = res.results[b]["yT"].T
    return out
